# version 10
# baseline (speedup 1.0000x reference)
from contextlib import ExitStack

import numpy as np
import ml_dtypes
import concourse.bass as bass
import concourse.mybir as mybir
from concourse.bass_utils import run_bass_kernel_spmd

F32 = mybir.dt.float32
BF16 = mybir.dt.bfloat16
FP16 = mybir.dt.float16
ALU = mybir.AluOpType
AFT = mybir.ActivationFunctionType
AX = mybir.AxisListType

T = 1024
D = 2048
HC = 16
OC = 64
NT = 8
NOG = 16
NDG = 8

C_ATT = float(np.float32(128.0) ** np.float32(-0.5))
B_ATT = float(np.float32(np.float32(128.0) ** np.float32(-0.5)) * np.float32(-176.0))
C_FWHT = float(np.float32(1.0 / np.sqrt(8192.0)))
EPS = float(np.float32(1e-5))
INV_D = float(np.float32(1.0 / 2048.0))
INV_NW = float(np.float32(1.0 / 16777216.0))
INV_128 = float(np.float32(1.0 / 128.0))

_STAGE = "full"


def _ladder(v, w0sl, valsl, btsl, outsl):
    for k in range(8, -1, -1):
        v.tensor_scalar(btsl, valsl, float(2 ** k), float(2 ** k), ALU.is_ge, ALU.mult)
        v.tensor_tensor(valsl, valsl, btsl, ALU.subtract)
    v.scalar_tensor_tensor(valsl, w0sl, -128.0, valsl, ALU.add, ALU.subtract)
    v.tensor_scalar(outsl, valsl, -128.0, 127.0, ALU.max, ALU.min)


def build_program(stage="full"):
    nc = bass.Bass("TRN2", target_bir_lowering=False,
                   detect_race_conditions=False)

    xd = nc.dram_tensor("xd", [T, D], F32, kind="ExternalInput").ap()
    xtr = nc.dram_tensor("xtr", [D, T], F32, kind="ExternalInput").ap()
    wgt = nc.dram_tensor("wgt", [NOG, HC, 128, 512], F32, kind="ExternalInput").ap()
    wut = nc.dram_tensor("wut", [NOG, HC, 128, 512], F32, kind="ExternalInput").ap()
    wdt = nc.dram_tensor("wdt", [NDG, OC, 128, 256], F32, kind="ExternalInput").ap()
    h128 = nc.dram_tensor("h128", [128, 128], BF16, kind="ExternalInput").ap()
    idn = nc.dram_tensor("idn", [128, 128], BF16, kind="ExternalInput").ap()
    idnf = nc.dram_tensor("idnf", [128, 128], F32, kind="ExternalInput").ap()
    ones_r = nc.dram_tensor("ones_r", [1, 128], F32, kind="ExternalInput").ap()
    ones_c = nc.dram_tensor("ones_c", [128, 1], F32, kind="ExternalInput").ap()

    out = nc.dram_tensor("out", [T, D], F32, kind="ExternalOutput").ap()

    a1_scr = nc.dram_tensor("a1_scr", [T, D], F32, kind="Internal").ap()
    b_scr = nc.dram_tensor("b_scr", [T, D], F32, kind="Internal").ap()
    i_scr = nc.dram_tensor("i_scr", [128, OC, T], F32, kind="Internal").ap()

    es0 = ExitStack()

    def sb(stk, name, shape, dtype):
        return stk.enter_context(nc.sbuf_tensor(name, shape, dtype))

    def pst(stk, name, shape, dtype):
        return stk.enter_context(nc.psum_tensor(name, shape, dtype))

    cst = sb(es0, "cst", [128, 6], F32)
    h128t = sb(es0, "h128t", [128, 128], BF16)
    idnt = sb(es0, "idnt", [128, 128], BF16)
    idnft = sb(es0, "idnft", [128, 128], F32)
    onesr = sb(es0, "onesr", [1, 128], F32)
    onesc = sb(es0, "onesc", [128, 1], F32)
    asum = sb(es0, "asum", [128, 4], F32)
    thb = sb(es0, "thb", [128, 3], F32)
    nthb = sb(es0, "nthb", [128, 3], F32)
    qv = sb(es0, "qv", [1, 4], F32)
    qgb = sb(es0, "qgb", [128, 1], F32)
    cattb = sb(es0, "cattb", [128, 1], F32)
    qub = sb(es0, "qub", [128, 1], F32)
    qdb = sb(es0, "qdb", [128, 1], F32)
    tiny = sb(es0, "tiny", [1, 8], F32)
    srx = sb(es0, "srx", [128, 8], F32)
    sry = sb(es0, "sry", [128, 8], F32)
    rqy = sb(es0, "rqy", [128, 8], F32)
    gxc = sb(es0, "gxc", [128, 8], F32)
    guc = sb(es0, "guc", [128, 8], F32)
    tcol = sb(es0, "tcol", [128, 4], F32)
    junk = sb(es0, "junk", [128, 256], F32)
    mx8 = sb(es0, "mx8", [128, 8], F32)
    sxv = sb(es0, "sxv", [128, 8], F32)

    nc.const_aps.register(F32, 128.5, cst[:, 0:1])
    nc.const_aps.register(F32, EPS, cst[:, 1:2])
    nc.const_aps.register(F32, INV_D, cst[:, 3:4])
    nc.const_aps.register(F32, B_ATT, cst[:, 4:5])

    es_cat = ExitStack()
    ps_cat = pst(es_cat, "ps_cat", [128, 1], F32)
    SEM = {}
    CNT = {}
    for nm in ("ld", "ld2", "ld3", "ld4", "ld5", "st", "pa", "pb", "pc", "da",
               "db", "dc", "aa", "gp", "gp2", "q", "q2", "mm", "tr", "ev", "si",
               "si2", "stt", "st2"):
        SEM[nm] = es0.enter_context(nc.semaphore("s_" + nm))
        CNT[nm] = 0

    def inc(nm, ins, k=1):
        ins.then_inc(SEM[nm], k)
        CNT[nm] += k
        return ins

    def sep(v):
        v.tensor_scalar(junk[:], junk[:], 1.0, None, ALU.mult)

    with nc.Block() as block:
        @block.vector
        def _(v):
            v.memset(cst[:, 0:1], 128.5)
            v.memset(cst[:, 1:2], EPS)
            v.memset(cst[:, 2:3], 128.0)
            v.memset(cst[:, 3:4], INV_D)
            v.memset(cst[:, 4:5], B_ATT)
            v.memset(asum[:, :], 0.0)
            v.memset(junk[:], 1.0)

        @block.sync
        def _(s):
            for src, dst in ((h128, h128t), (idn, idnt), (idnf, idnft),
                             (ones_r, onesr), (ones_c, onesc)):
                inc("ld", s.dma_start(dst[:], src), 16)
            s.wait_ge(SEM["ld"], CNT["ld"])

    with nc.Block() as block:
        @block.vector
        def _(v):
            v.tensor_copy(tiny[:, 7:8], cst[0:1, 2:3])

    with nc.Block() as block:
        @block.scalar
        def _(sc):
            sc.activation(tiny[:, 7:8], tiny[:, 7:8], AFT.Sqrt, bias=0.0, scale=1.0)

    with nc.Block() as block:
        @block.vector
        def _(v):
            v.reciprocal(tiny[:, 7:8], tiny[:, 7:8])

    with nc.Block() as block:
        @block.tensor
        def _(t):
            t.matmul(ps_cat[:], onesr[:], tiny[:, 7:8], start=True, stop=True)

    with nc.Block() as block:
        @block.vector
        def _(v):
            v.tensor_copy(cattb[:], ps_cat[:])
    es_cat.close()

    es_x = ExitStack()
    xqT = sb(es_x, "xqT", [128, HC * T], BF16)

    es1 = ExitStack()
    xdt = sb(es1, "xdt", [128, NT * D], F32)
    xtb = sb(es1, "xtb", [128, HC * T], BF16)
    xdb = sb(es1, "xdb", [128, NT * D], BF16)
    et = sb(es1, "et", [128, 2 * 256], BF16)
    ett = sb(es1, "ett", [128, 2 * 256], BF16)
    rsum = sb(es1, "rsum", [128, 4], F32)
    lsc = sb(es1, "lsc", [128, D], F32)
    mw = sb(es1, "mw", [128, 2 * 2048], F32)
    red = sb(es1, "red", [128, 384], F32)
    l16a = sb(es1, "l16a", [128, 2048], FP16)
    l16b = sb(es1, "l16b", [128, 2048], FP16)
    l16c = sb(es1, "l16c", [128, 2048], FP16)
    ps_s = pst(es1, "ps_s", [128, 2 * 256], F32)
    ps_tr = pst(es1, "ps_tr", [128, 4 * 128], BF16)
    ps_o = pst(es1, "ps_o", [128, 2 * 128], F32)

    mn_si0 = CNT["si"]
    mn_si20 = CNT["si2"]
    MQ = ("ld", "ld2", "ld3", "ld4")
    mn_l = {}
    with nc.Block() as block:
        @block.sync
        def _(s):
            for j in range(NT):
                inc("ld", s.dma_start(xdt[:, j * D:(j + 1) * D],
                                      xd[j * 128:(j + 1) * 128, :]), 16)
            s.wait_ge(SEM["ld"], CNT["ld"])
            for q in MQ:
                mn_l[q] = CNT[q]

        @block.gpsimd
        def _(g):
            for j in range(HC):
                inc("ld5", g.dma_start(xtb[:, j * T:(j + 1) * T],
                                       xtr[j * 128:(j + 1) * 128, :]), 16)
            for j in range(NT):
                inc("ld5", g.dma_start(xdb[:, j * D:(j + 1) * D],
                                       xd[j * 128:(j + 1) * 128, :]), 16)
            g.wait_ge(SEM["ld5"], CNT["ld5"])

    ablocks = [(sg, pr, hh) for sg in range(2) for pr in range(2) for hh in range(8)]
    pa0, pb0, pc0 = CNT["pa"], CNT["pb"], CNT["pc"]
    da0, db0, dc0 = CNT["da"], CNT["db"], CNT["dc"]
    aa0 = CNT["aa"]
    with nc.Block() as block:
        @block.sync
        def _(s):
            k = 0
            for w in (wgt, wut, wdt):
                for j in range(16):
                    for qq in range(8):
                        if k >= 4:
                            kk = k - 4
                            if kk % 2 == 0:
                                s.wait_ge(SEM["si"], mn_si0 + kk // 2 + 1)
                            else:
                                s.wait_ge(SEM["si2"], mn_si20 + kk // 2 + 1)
                        if w is wdt:
                            wsrc = w[j // 2, (j % 2) * 32 + 4 * qq:
                                     (j % 2) * 32 + 4 * qq + 4].transpose([1, 0, 2])
                            na = 4
                        else:
                            wsrc = w[j, 2 * qq:2 * qq + 2].transpose([1, 0, 2])
                            na = 2
                        dst = mw[:, (k % 4) * 1024:(k % 4 + 1) * 1024].rearrange(
                            "p (a c) -> p a c", a=na)
                        inc(MQ[k % 4], s.dma_start(dst, wsrc), 16)
                        k += 1

        @block.tensor
        def _(t):
            for i, (sg, pr, hh) in enumerate(ablocks):
                head = 2 * hh + pr
                base = 512 * sg + 256 * pr
                gt = xtb[:, head * T + base: head * T + base + 256]
                if i > 0:
                    t.wait_ge(SEM["aa"], aa0 + i)
                for qc in range(2):
                    ins = t.matmul(ps_s[:, qc * 256:(qc + 1) * 256],
                                   gt[:, qc * 128:(qc + 1) * 128], gt,
                                   start=True, stop=True)
                ins.then_inc(SEM["pa"], 1)
                t.wait_ge(SEM["aa"], aa0 + i + 1)
                if i > 0:
                    t.wait_ge(SEM["db"], db0 + i)
                for kc in range(2):
                    for qc in range(2):
                        z = 2 * kc + qc
                        ins = t.transpose(ps_tr[:, z * 128:(z + 1) * 128],
                                          et[:, qc * 256 + kc * 128:
                                             qc * 256 + kc * 128 + 128], idnt[:])
                ins.then_inc(SEM["pb"], 1)
                t.wait_ge(SEM["db"], db0 + i + 1)
                if i > 0:
                    t.wait_ge(SEM["dc"], dc0 + i)
                tb4 = base // 128
                for qc in range(2):
                    for kc in range(2):
                        gk = xdb[:, (tb4 + kc) * D + head * 128:
                                 (tb4 + kc) * D + head * 128 + 128]
                        ins = t.matmul(ps_o[:, qc * 128:(qc + 1) * 128],
                                       ett[:, kc * 256 + qc * 128:
                                           kc * 256 + qc * 128 + 128],
                                       gk, start=(kc == 0), stop=(kc == 1))
                ins.then_inc(SEM["pc"], 1)

        @block.vector
        def _(v):
            kcv = [1]

            def do_chunks_v(n):
                for _ in range(n):
                    k = kcv[0]
                    v.wait_ge(SEM[MQ[k % 4]],
                              mn_l[MQ[k % 4]] + 16 * (k // 4 + 1))
                    buf = mw[:, (k % 4) * 1024:(k % 4 + 1) * 1024]
                    ins = v.tensor_reduce(red[:, k:k + 1], buf, AX.X, ALU.add,
                                          apply_absolute_value=True)
                    ins.then_inc(SEM["si2"], 1)
                    kcv[0] += 2

            for i in range(32):
                sg, pr, hh = ablocks[i]
                head = 2 * hh + pr
                base = 512 * sg + 256 * pr
                v.wait_ge(SEM["pb"], pb0 + i + 1)
                for kc in range(2):
                    for qc in range(2):
                        z = 2 * kc + qc
                        ins = v.tensor_copy(ett[:, kc * 256 + qc * 128:
                                                kc * 256 + qc * 128 + 128],
                                            ps_tr[:, z * 128:(z + 1) * 128])
                ins.then_inc(SEM["db"], 1)
                v.wait_ge(SEM["pc"], pc0 + i + 1)
                tb4 = base // 128
                for qc in range(2):
                    v.reciprocal(rsum[:, qc:qc + 1], rsum[:, qc:qc + 1])
                sep(v)
                for qc in range(2):
                    dst = xdt[:, (tb4 + qc) * D + head * 128:
                              (tb4 + qc) * D + head * 128 + 128]
                    ins = v.scalar_tensor_tensor(dst, ps_o[:, qc * 128:(qc + 1) * 128],
                                                 rsum[:, qc:qc + 1], dst,
                                                 ALU.mult, ALU.add)
                ins.then_inc(SEM["dc"], 1)
                do_chunks_v(6)

        @block.scalar
        def _(sc):
            kch = [0]

            def do_chunks(n):
                for _ in range(n):
                    k = kch[0]
                    sc.wait_ge(SEM[MQ[k % 4]],
                               mn_l[MQ[k % 4]] + 16 * (k // 4 + 1))
                    buf = mw[:, (k % 4) * 1024:(k % 4 + 1) * 1024]
                    ins = sc.activation(buf, buf, AFT.Abs, bias=0.0,
                                        scale=1.0, accum_out=red[:, k:k + 1])
                    ins.then_inc(SEM["si"], 1)
                    kch[0] += 2

            for i in range(32):
                sc.wait_ge(SEM["pa"], pa0 + i + 1)
                if i > 0:
                    sc.wait_ge(SEM["pb"], pb0 + i)
                for qc in range(2):
                    ins = sc.activation(et[:, qc * 256:(qc + 1) * 256],
                                        ps_s[:, qc * 256:(qc + 1) * 256],
                                        AFT.Exp, bias=B_ATT,
                                        scale=cattb[:, 0:1],
                                        accum_out=rsum[:, qc:qc + 1])
                ins.then_inc(SEM["aa"], 1)
                do_chunks(6)
            CNT["si"] = mn_si0 + 192
            CNT["si2"] = mn_si20 + 192
    CNT["pa"] = pa0 + 32
    CNT["pb"] = pb0 + 32
    CNT["pc"] = pc0 + 32
    CNT["db"] = db0 + 32
    CNT["dc"] = dc0 + 32
    CNT["aa"] = aa0 + 32

    with nc.Block() as block:
        @block.vector
        def _(v):
            for j in range(128):
                for mi in range(3):
                    v.tensor_tensor(asum[:, mi:mi + 1], asum[:, mi:mi + 1],
                                    red[:, mi * 128 + j:mi * 128 + j + 1],
                                    ALU.add)

    es_f = ExitStack()
    ps_fin = pst(es_f, "ps_fin", [128, 8], F32)
    f_pa0, f_da0, f_aa0 = CNT["pa"], CNT["da"], CNT["aa"]
    with nc.Block() as block:
        @block.tensor
        def _(t):
            for mi in range(3):
                ins = t.matmul(ps_fin[0:1, mi:mi + 1], asum[:, mi:mi + 1],
                               onesc[:], start=True, stop=True)
            ins.then_inc(SEM["pa"], 1)
            t.wait_ge(SEM["da"], f_da0 + 2)
            for mi in range(3):
                t.matmul(ps_fin[:, mi:mi + 1], onesr[:], tiny[:, mi:mi + 1],
                         start=True, stop=True)
            for z in range(3):
                ins = t.matmul(ps_fin[:, 3 + z:4 + z], onesr[:], qv[:, z:z + 1],
                               start=True, stop=True)
            ins.then_inc(SEM["pa"], 1)

        @block.vector
        def _(v):
            v.wait_ge(SEM["pa"], f_pa0 + 1)
            for mi in range(3):
                v.tensor_copy(tiny[:, mi:mi + 1], ps_fin[0:1, mi:mi + 1])
            sep(v)
            for mi in range(3):
                v.tensor_scalar(tiny[:, mi:mi + 1], tiny[:, mi:mi + 1],
                                INV_NW, EPS, ALU.mult, ALU.max)
            sep(v)
            for mi in range(3):
                v.reciprocal(tiny[:, 4 + mi:5 + mi], tiny[:, mi:mi + 1])
            sep(v)
            for mi in range(3):
                v.reciprocal(qv[:, mi:mi + 1], tiny[:, 4 + mi:5 + mi])
            sep(v)
            for mi in range(3):
                ins = v.tensor_scalar(tiny[:, mi:mi + 1], tiny[:, mi:mi + 1],
                                      0.5, None, ALU.mult)
            ins.then_inc(SEM["da"], 1)
            ins = v.tensor_scalar(junk[:, 0:1], junk[:, 0:1], 1.0, None, ALU.add)
            ins.then_inc(SEM["da"], 1)
            v.wait_ge(SEM["pa"], f_pa0 + 2)
            for mi in range(3):
                v.tensor_copy(thb[:, mi:mi + 1], ps_fin[:, mi:mi + 1])
                v.tensor_scalar(nthb[:, mi:mi + 1], ps_fin[:, mi:mi + 1],
                                -1.0, None, ALU.mult)
            v.tensor_copy(qgb[:], ps_fin[:, 3:4])
            v.tensor_copy(qub[:], ps_fin[:, 4:5])
            v.tensor_copy(qdb[:], ps_fin[:, 5:6])

    CNT["pa"] = f_pa0 + 2
    CNT["da"] = f_da0 + 2
    es_f.close()


    if stage == "attn":
        with nc.Block() as block:
            @block.sync
            def _(s):
                for j in range(NT):
                    inc("st", s.dma_start(out[j * 128:(j + 1) * 128, :],
                                          xdt[:, j * D:(j + 1) * D]), 16)
                s.wait_ge(SEM["st"], CNT["st"])
        es1.close(); es_x.close(); es0.close()
        return nc

    ln_d0, ln_a0 = CNT["da"], CNT["aa"]
    with nc.Block() as block:
        @block.vector
        def _(v):
            for t in range(NT // 2):
                t0 = xdt[:, (2 * t) * D:(2 * t + 1) * D]
                t1 = xdt[:, (2 * t + 1) * D:(2 * t + 2) * D]
                v.reduce_sum(tcol[:, 0:1], t0, AX.X)
                v.reduce_sum(tcol[:, 1:2], t1, AX.X)
                v.tensor_scalar(tcol[:, 0:1], tcol[:, 0:1], INV_D, None, ALU.mult)
                v.tensor_scalar(tcol[:, 1:2], tcol[:, 1:2], INV_D, None, ALU.mult)
                sep(v)
                inc("da", v.tensor_scalar(t0, t0, tcol[:, 0:1], None, ALU.subtract))
                inc("da", v.tensor_scalar(t1, t1, tcol[:, 1:2], None, ALU.subtract))
                v.wait_ge(SEM["aa"], ln_a0 + 2 * (t + 1))
                v.reciprocal(tcol[:, 2:3], tcol[:, 2:3])
                v.reciprocal(tcol[:, 3:4], tcol[:, 3:4])
                sep(v)
                inc("da", v.tensor_scalar(t0, t0, tcol[:, 2:3], None, ALU.mult))
                inc("da", v.tensor_scalar(t1, t1, tcol[:, 3:4], None, ALU.mult))

        @block.scalar
        def _(sc):
            for t in range(NT // 2):
                t0 = xdt[:, (2 * t) * D:(2 * t + 1) * D]
                t1 = xdt[:, (2 * t + 1) * D:(2 * t + 2) * D]
                sc.wait_ge(SEM["da"], ln_d0 + 4 * t + 1)
                sc.activation(lsc[:], t0, AFT.Square, bias=0.0, scale=1.0,
                              accum_out=tcol[:, 2:3])
                sc.wait_ge(SEM["da"], ln_d0 + 4 * t + 2)
                sc.activation(lsc[:], t1, AFT.Square, bias=0.0, scale=1.0,
                              accum_out=tcol[:, 3:4])
                ins = sc.activation(tcol[:, 2:3], tcol[:, 2:3], AFT.Sqrt,
                                    bias=EPS, scale=INV_D)
                ins.then_inc(SEM["aa"], 1)
                ins = sc.activation(tcol[:, 3:4], tcol[:, 3:4], AFT.Sqrt,
                                    bias=EPS, scale=INV_D)
                ins.then_inc(SEM["aa"], 1)
                sc.wait_ge(SEM["da"], ln_d0 + 4 * t + 4)
        CNT["aa"] += NT

    with nc.Block() as block:
        @block.gpsimd
        def _(g):
            for j in range(NT):
                inc("st2", g.dma_start(a1_scr[j * 128:(j + 1) * 128, :],
                                       xdt[:, j * D:(j + 1) * D]), 16)
            g.wait_ge(SEM["st2"], CNT["st2"])

    if stage == "a1":
        with nc.Block() as block:
            @block.sync
            def _(s):
                for j in range(NT):
                    inc("st", s.dma_start(out[j * 128:(j + 1) * 128, :],
                                          xdt[:, j * D:(j + 1) * D]), 16)
                s.wait_ge(SEM["st"], CNT["st"])
        es1.close(); es_x.close(); es0.close()
        return nc

    xq_d0, xq_a0 = CNT["da"], CNT["aa"]
    with nc.Block() as block:
        @block.vector
        def _(v):
            for j in range(NT):
                v.reduce_max(mx8[:, j:j + 1], xdt[:, j * D:(j + 1) * D], AX.X,
                             apply_absolute_value=True)
            for j in range(NT):
                v.tensor_scalar(srx[:, j:j + 1], mx8[:, j:j + 1], EPS, INV_128,
                                ALU.max, ALU.mult)
            for j in range(NT):
                v.reciprocal(sxv[:, j:j + 1], srx[:, j:j + 1])
            for j in range(NT):
                v.tensor_tensor(gxc[:, j:j + 1], srx[:, j:j + 1], qgb[:], ALU.mult)
            for j in range(NT):
                ins = v.tensor_tensor(guc[:, j:j + 1], srx[:, j:j + 1], qub[:],
                                      ALU.mult)
            inc("da", ins)
            for j in range(NT):
                tile = xdt[:, j * D:(j + 1) * D]
                v.tensor_scalar(l16a[:], tile, sxv[:, j:j + 1], 128.5,
                                ALU.mult, ALU.add)
                v.tensor_scalar(l16b[:], tile, sxv[:, j:j + 1], 128.5,
                                ALU.mult, ALU.add)
                _ladder(v, l16a[:], l16b[:], l16c[:], xdb[:, j * D:(j + 1) * D])

    tr_p0, tr_d0 = CNT["pa"], CNT["da"]
    es_tp = ExitStack()
    ps_x = pst(es_tp, "ps_x", [128, 2 * 1024], BF16)
    with nc.Block() as block:
        @block.tensor
        def _(t):
            g = 0
            for j in range(NT):
                for hcg in range(4):
                    if g >= 2:
                        t.wait_ge(SEM["da"], tr_d0 + g - 1)
                    for z in range(4):
                        hc = hcg * 4 + z
                        ins = t.transpose(
                            ps_x[:, (g % 2) * 1024 + z * 128:
                                 (g % 2) * 1024 + (z + 1) * 128],
                            xdb[:, j * D + hc * 128: j * D + hc * 128 + 128], idnt[:])
                    ins.then_inc(SEM["pa"], 1)
                    g += 1

        @block.vector
        def _(v):
            g = 0
            for j in range(NT):
                for hcg in range(4):
                    v.wait_ge(SEM["pa"], tr_p0 + g + 1)
                    for z in range(4):
                        hc = hcg * 4 + z
                        ins = v.tensor_copy(
                            xqT[:, hc * T + j * 128: hc * T + (j + 1) * 128],
                            ps_x[:, (g % 2) * 1024 + z * 128:
                                 (g % 2) * 1024 + (z + 1) * 128])
                    ins.then_inc(SEM["da"], 1)
                    g += 1
    CNT["pa"] = tr_p0 + 32
    CNT["da"] = tr_d0 + 32
    es_tp.close()

    if stage == "xq":
        with nc.Block() as block:
            @block.vector
            def _(v):
                v.tensor_copy(lsc[:, 0:1024], xqT[:, 0:T])
        with nc.Block() as block:
            @block.sync
            def _(s):
                inc("st", s.dma_start(out[0:128, 0:1024], lsc[:, 0:1024]), 16)
                s.wait_ge(SEM["st"], CNT["st"])
        es1.close(); es_x.close(); es0.close()
        return nc

    es1.close()

    es2 = ExitStack()
    wfg = sb(es2, "wfg", [128, 2 * 2048], F32)
    wfu = sb(es2, "wfu", [128, 2 * 2048], F32)
    wqg = sb(es2, "wqg", [128, 2 * 8192], BF16)
    wqu = sb(es2, "wqu", [128, 2 * 8192], BF16)
    tpb = sb(es2, "tpb", [128, 2048], BF16)
    tnb = sb(es2, "tnb", [128, 2048], BF16)
    sgt = sb(es2, "sgt", [128, 2 * T], F32)
    ist = sb(es2, "ist", [128, 2 * 4 * T], F32)
    grwx = sb(es2, "grwx", [1, T], F32)
    gxB = sb(es2, "gxB", [128, T], F32)
    ps_g = pst(es2, "ps_g", [128, 2 * T], F32)
    ps_u = pst(es2, "ps_u", [128, 2 * T], F32)

    gx_p0, gx_d0 = CNT["pa"], CNT["da"]
    with nc.Block() as block:
        @block.tensor
        def _(t):
            for j in range(NT):
                ins = t.transpose(ps_g[0:1, j * 128:(j + 1) * 128],
                                  srx[:, j:j + 1], idnft[:])
            ins.then_inc(SEM["pa"], 1)
            t.wait_ge(SEM["da"], gx_d0 + 1)
            for hf in range(2):
                ins = t.matmul(ps_u[:, hf * 512:(hf + 1) * 512], onesr[:],
                               grwx[0:1, hf * 512:(hf + 1) * 512],
                               start=True, stop=True)
            ins.then_inc(SEM["pa"], 1)

        @block.vector
        def _(v):
            v.wait_ge(SEM["pa"], gx_p0 + 1)
            for j in range(NT):
                v.tensor_copy(grwx[0:1, j * 128:(j + 1) * 128],
                              ps_g[0:1, j * 128:(j + 1) * 128])
            sep(v)
            ins = v.tensor_scalar(grwx[0:1, :], grwx[0:1, :], qv[:, 0:1],
                                  None, ALU.mult)
            ins.then_inc(SEM["da"], 1)
            v.wait_ge(SEM["pa"], gx_p0 + 2)
            v.tensor_copy(gxB[:, 0:512], ps_u[:, 0:512])
            v.tensor_copy(gxB[:, 512:1024], ps_u[:, 512:1024])
    CNT["pa"] = gx_p0 + 2
    CNT["da"] = gx_d0 + 1

    g_l = {nm: CNT[nm] for nm in ("ld", "ld2", "ld3", "ld4")}
    g_q0, g_q20 = CNT["q"], CNT["q2"]
    g_mm0, g_si0, g_ev0 = CNT["mm"], CNT["si"], CNT["ev"]
    g_gp0, g_gp20 = CNT["gp"], CNT["gp2"]
    g_gr0 = CNT["dc"]
    with nc.Block() as block:
        @block.sync
        def _(s):
            for og in range(NOG):
                for qq in range(4):
                    gq = og * 4 + qq
                    if gq >= 2:
                        s.wait_ge(SEM["q"], g_q0 + gq - 1)
                    src = wgt[og, 4 * qq:4 * qq + 4].transpose([1, 0, 2])
                    dst = wfg[:, (gq % 2) * 2048:(gq % 2 + 1) * 2048].rearrange(
                        "p (a c) -> p a c", a=4)
                    inc(("ld", "ld2")[qq % 2], s.dma_start(dst, src), 16)
                for qq in range(4):
                    gq = og * 4 + qq
                    if gq >= 2:
                        s.wait_ge(SEM["q2"], g_q20 + gq - 1)
                    src = wut[og, 4 * qq:4 * qq + 4].transpose([1, 0, 2])
                    dst = wfu[:, (gq % 2) * 2048:(gq % 2 + 1) * 2048].rearrange(
                        "p (a c) -> p a c", a=4)
                    inc(("ld3", "ld4")[qq % 2], s.dma_start(dst, src), 16)

        @block.vector
        def _(v):
            for og in range(NOG + 1):
                if og < NOG:
                    if og >= 2:
                        v.wait_ge(SEM["mm"], g_mm0 + (og - 1) * 4)
                    for half in range(2):
                        qn = "q" if half == 0 else "q2"
                        buf = wfg if half == 0 else wfu
                        wq = wqg if half == 0 else wqu
                        for qq in range(4):
                            gq = og * 4 + qq
                            lnm = (("ld", "ld2") if half == 0 else ("ld3", "ld4"))[qq % 2]
                            v.wait_ge(SEM[lnm], g_l[lnm] + 16 * (og * 2 + qq // 2 + 1))
                            bsl = buf[:, (gq % 2) * 2048:(gq % 2 + 1) * 2048]
                            qsl = wq[:, (og % 2) * 8192 + qq * 2048:
                                     (og % 2) * 8192 + (qq + 1) * 2048]
                            v.tensor_scalar(tpb[:], bsl, thb[:, half:half + 1],
                                            None, ALU.is_gt)
                            v.tensor_scalar(tnb[:], bsl, nthb[:, half:half + 1],
                                            None, ALU.is_lt)
                            inc(qn, v.tensor_tensor(qsl, tpb[:], tnb[:], ALU.subtract))
                if og >= 1:
                    po = og - 1
                    for oc4 in range(4):
                        m = po * 4 + oc4
                        p = m % 2
                        v.wait_ge(SEM["mm"], g_mm0 + m + 1)
                        ins = v.tensor_tensor(sgt[:, p * T:p * T + T],
                                              ps_g[:, p * T:p * T + T],
                                              gxB[:], ALU.mult)
                        inc("dc", ins)
                        v.wait_ge(SEM["si"], g_si0 + m + 1)
                        if po >= 2:
                            v.wait_ge(SEM[("gp", "gp2")[po % 2]],
                                      (g_gp0, g_gp20)[po % 2] + 64 * ((po - 2) // 2 + 1))
                        ins = v.tensor_tensor(
                            ist[:, (po % 2) * 4 * T + oc4 * T:
                                (po % 2) * 4 * T + (oc4 + 1) * T],
                            sgt[:, p * T:p * T + T],
                            ps_u[:, p * T:p * T + T], ALU.mult)
                        inc("ev", ins)

        @block.tensor
        def _(t):
            for og in range(NOG):
                t.wait_ge(SEM["q"], g_q0 + og * 4 + 4)
                t.wait_ge(SEM["q2"], g_q20 + og * 4 + 4)
                for oc4 in range(4):
                    m = og * 4 + oc4
                    p = m % 2
                    if m >= 2:
                        t.wait_ge(SEM["ev"], g_ev0 + m - 1)
                    for half in range(2):
                        wq = wqg if half == 0 else wqu
                        pdst = ps_g if half == 0 else ps_u
                        for hc in range(HC):
                            lh = wq[:, (og % 2) * 8192 + hc * 512 + oc4 * 128:
                                    (og % 2) * 8192 + hc * 512 + oc4 * 128 + 128]
                            ins = t.matmul(pdst[:, p * T:p * T + 512], lh,
                                           xqT[:, hc * T:hc * T + 512],
                                           start=(hc == 0), stop=(hc == HC - 1))
                            ins = t.matmul(pdst[:, p * T + 512:p * T + 1024], lh,
                                           xqT[:, hc * T + 512:hc * T + 1024],
                                           start=(hc == 0), stop=(hc == HC - 1))
                    ins.then_inc(SEM["mm"], 1)

        @block.scalar
        def _(sc):
            for og in range(NOG):
                for oc4 in range(4):
                    m = og * 4 + oc4
                    p = m % 2
                    sc.wait_ge(SEM["dc"], g_gr0 + m + 1)
                    ins = sc.activation(sgt[:, p * T:p * T + T],
                                        sgt[:, p * T:p * T + T],
                                        AFT.Silu, bias=0.0, scale=1.0)
                    ins.then_inc(SEM["si"], 1)

        @block.gpsimd
        def _(g):
            for og in range(NOG):
                g.wait_ge(SEM["ev"], g_ev0 + og * 4 + 4)
                for z in range(4):
                    inc(("gp", "gp2")[og % 2], g.dma_start(
                        i_scr[:, og * 4 + z:og * 4 + z + 1, :].squeeze(1),
                        ist[:, (og % 2) * 4 * T + z * T:
                            (og % 2) * 4 * T + (z + 1) * T]), 16)
            g.wait_ge(SEM["gp"], CNT["gp"])
            g.wait_ge(SEM["gp2"], CNT["gp2"])
    CNT["q"] = g_q0 + 64
    CNT["q2"] = g_q20 + 64
    CNT["mm"] = g_mm0 + 64
    CNT["si"] = g_si0 + 64
    CNT["dc"] = g_gr0 + 64
    CNT["ev"] = g_ev0 + 64
    es2.close()
    es_x.close()

    es3a = ExitStack()
    yqT = sb(es3a, "yqT", [128, OC * T], BF16)
    es3 = ExitStack()
    ibuf = sb(es3, "ibuf", [128, OC * 128], F32)
    l16a = sb(es3, "m16a", [128, 2048], FP16)
    l16b = sb(es3, "m16b", [128, 2048], FP16)
    l16c = sb(es3, "m16c", [128, 2048], FP16)
    hib = sb(es3, "hib", [128, 2 * 512], BF16)
    lob = sb(es3, "lob", [128, 2 * 512], BF16)
    mby = sb(es3, "mby", [128, 128], F32)
    csr = sb(es3, "csr", [1, 128], F32)
    scb = sb(es3, "scb", [128, 128], F32)
    ps_z = pst(es3, "ps_z", [128, 2 * 512], F32)
    ps_m = pst(es3, "ps_m", [128, 128], F32)
    ps_r = pst(es3, "ps_r", [1, 128], F32)
    ps_b = pst(es3, "ps_b", [128, 128], F32)

    fw_ld0 = CNT["ld"]
    fw_ev0 = CNT["ev"]
    for tb in range(NT):
        if stage == "fw1" and tb == 1:
            with nc.Block() as block:
                @block.sync
                def _(s):
                    for ch in range(4):
                        inc("st", s.dma_start(out[ch * 128:(ch + 1) * 128, 0:2048],
                                              ibuf[:, ch * 2048:(ch + 1) * 2048]), 16)
                    s.wait_ge(SEM["st"], CNT["st"])
            es3.close(); es3a.close(); es0.close()
            return nc

        f_q0, f_p0, f_e0 = CNT["q"], CNT["pa"], CNT["db"]
        f_pb0, f_dc0 = CNT["pb"], CNT["dc"]
        with nc.Block() as block:
            @block.sync
            def _(s):
                if tb >= 1:
                    s.wait_ge(SEM["ev"], fw_ev0 + tb)
                inc("ld", s.dma_start(
                    ibuf[:].rearrange("p (a t) -> p a t", a=OC),
                    i_scr[:, :, tb * 128:(tb + 1) * 128]), 16)

            @block.vector
            def _(v):
                v.wait_ge(SEM["ld"], CNT["ld"])
                for st_ in range(6):
                    ngrp = OC // (2 ** (st_ + 1))
                    vw = ibuf[:].rearrange("p (g two l) -> p g two l",
                                           g=ngrp, two=2)
                    lo = vw[:, :, 0, :]
                    hi = vw[:, :, 1, :]
                    v.tensor_tensor(lo, lo, hi, ALU.add)
                    v.scalar_tensor_tensor(hi, hi, -2.0, lo, ALU.mult, ALU.add)
                for gg in range(16):
                    if gg >= 2:
                        v.wait_ge(SEM["pa"], f_p0 + gg - 1)
                    sl = ibuf[:, gg * 512:(gg + 1) * 512]
                    hsl = hib[:, (gg % 2) * 512:(gg % 2 + 1) * 512]
                    lsl = lob[:, (gg % 2) * 512:(gg % 2 + 1) * 512]
                    v.tensor_copy(hsl, sl)
                    inc("q", v.tensor_tensor(lsl, sl, hsl, ALU.subtract))
                    if gg >= 1:
                        v.wait_ge(SEM["pa"], f_p0 + gg)
                        pg = gg - 1
                        inc("db", v.tensor_copy(
                            ibuf[:, pg * 512:(pg + 1) * 512],
                            ps_z[:, (pg % 2) * 512:(pg % 2 + 1) * 512]))
                v.wait_ge(SEM["pa"], f_p0 + 16)
                inc("db", v.tensor_copy(ibuf[:, 15 * 512:16 * 512],
                                        ps_z[:, 512:1024]))
                ins = v.tensor_reduce(mby[:], ibuf[:].rearrange(
                    "p (a t) -> p t a", a=OC), AX.X, ALU.max,
                    apply_absolute_value=True)
                inc("dc", ins)
                v.wait_ge(SEM["pb"], f_pb0 + 1)
                v.reduce_max(tcol[:, 0:1], ps_m[:], AX.X)
                sep(v)
                v.tensor_tensor(tcol[:, 0:1], tcol[:, 0:1], guc[:, tb:tb + 1],
                                ALU.mult)
                sep(v)
                v.tensor_scalar(tcol[:, 0:1], tcol[:, 0:1], C_FWHT, EPS,
                                ALU.mult, ALU.max)
                sep(v)
                v.tensor_scalar(sry[:, tb:tb + 1], tcol[:, 0:1], INV_128, None,
                                ALU.mult)
                sep(v)
                v.tensor_tensor(rqy[:, tb:tb + 1], sry[:, tb:tb + 1], qdb[:],
                                ALU.mult)
                sep(v)
                v.reciprocal(tcol[:, 1:2], sry[:, tb:tb + 1])
                sep(v)
                v.tensor_scalar(tcol[:, 1:2], tcol[:, 1:2], C_FWHT, None,
                                ALU.mult)
                sep(v)
                ins = v.tensor_tensor(tcol[:, 1:2], tcol[:, 1:2], guc[:, tb:tb + 1],
                                      ALU.mult)
                inc("dc", ins)
                v.wait_ge(SEM["pb"], f_pb0 + 2)
                inc("dc", v.tensor_copy(csr[:], ps_r[:]))
                v.wait_ge(SEM["pb"], f_pb0 + 3)
                v.tensor_copy(scb[:], ps_b[:])
                for z in range(4):
                    sl = ibuf[:, z * 2048:(z + 1) * 2048]
                    slv = sl.rearrange("p (a t) -> p a t", a=16)
                    bcast = scb[:].unsqueeze(1).broadcast_to([128, 16, 128])
                    v.tensor_tensor(slv, slv, bcast, ALU.mult)
                    v.tensor_scalar(l16a[:], sl, 128.5, None, ALU.add)
                    v.tensor_scalar(l16b[:], sl, 128.5, None, ALU.add)
                    yout = yqT[:, tb * 8192 + z * 2048: tb * 8192 + (z + 1) * 2048]
                    _ladder(v, l16a[:], l16b[:], l16c[:], yout)
                inc("ev", v.nop())

            @block.tensor
            def _(t):
                for gg in range(16):
                    t.wait_ge(SEM["q"], f_q0 + gg + 1)
                    if gg >= 2:
                        t.wait_ge(SEM["db"], f_e0 + gg - 1)
                    hsl = hib[:, (gg % 2) * 512:(gg % 2 + 1) * 512]
                    lsl = lob[:, (gg % 2) * 512:(gg % 2 + 1) * 512]
                    t.matmul(ps_z[:, (gg % 2) * 512:(gg % 2 + 1) * 512],
                             h128t[:], hsl, start=True, stop=False)
                    ins = t.matmul(ps_z[:, (gg % 2) * 512:(gg % 2 + 1) * 512],
                                   h128t[:], lsl, start=False, stop=True)
                    ins.then_inc(SEM["pa"], 1)
                t.wait_ge(SEM["dc"], f_dc0 + 1)
                ins = t.transpose(ps_m[:], mby[:], idnft[:])
                ins.then_inc(SEM["pb"], 1)
                t.wait_ge(SEM["dc"], f_dc0 + 2)
                ins = t.transpose(ps_r[:], tcol[:, 1:2], idnft[:])
                ins.then_inc(SEM["pb"], 1)
                t.wait_ge(SEM["dc"], f_dc0 + 3)
                ins = t.matmul(ps_b[:], onesr[:], csr[:], start=True, stop=True)
                ins.then_inc(SEM["pb"], 1)
        CNT["q"] = f_q0 + 16
        CNT["pa"] = f_p0 + 16
        CNT["db"] = f_e0 + 16
        CNT["pb"] = f_pb0 + 3
        CNT["dc"] = f_dc0 + 3
        CNT["ev"] = fw_ev0 + tb + 1

    if stage == "yq":
        with nc.Block() as block:
            @block.vector
            def _(v):
                v.tensor_copy(ibuf[:, 4096:6144], yqT[:, 0:2048])
                v.tensor_copy(ibuf[:, 0:2048], yqT[:, 2048:4096])
        with nc.Block() as block:
            @block.sync
            def _(s):
                inc("st", s.dma_start(out[0:128, 0:2048], ibuf[:, 4096:6144]), 16)
                inc("st", s.dma_start(out[128:256, 0:2048], ibuf[:, 0:2048]), 16)
                inc("st", s.dma_start(out[256:384, 0:8], sry[:, :]), 16)
                s.wait_ge(SEM["st"], CNT["st"])
        es3.close(); es3a.close(); es0.close()
        return nc

    es3.close()

    es4 = ExitStack()
    dfs = sb(es4, "dfs", [128, 2 * 2048], F32)
    wqd = sb(es4, "wqd", [128, OC * 256], BF16)
    dtn = sb(es4, "dtn", [128, 2048], BF16)
    a1s = sb(es4, "a1s", [128, NT * 256], F32)
    bst = sb(es4, "bst", [128, NT * 256], F32)
    ps_d = pst(es4, "ps_d", [128, 2 * 512], F32)

    d_l = {nm: CNT[nm] for nm in ("ld", "ld2", "ld3")}
    d_q0, d_mm0, d_stt0, d_gp0 = (CNT["q"], CNT["mm"], CNT["stt"], CNT["gp"])
    with nc.Block() as block:
        @block.sync
        def _(s):
            for og in range(NDG):
                if og >= 1:
                    s.wait_ge(SEM["stt"], d_stt0 + og * 8)
                inc("ld3", s.dma_start(
                    a1s[:],
                    a1_scr.rearrange("(j p) c -> p j c", p=128)[
                        :, :, og * 256:(og + 1) * 256]), 16)
                for ss in range(NT):
                    gq = og * 8 + ss
                    if gq >= 2:
                        s.wait_ge(SEM["q"], d_q0 + gq - 1)
                    src = wdt[og, 8 * ss:8 * ss + 8].transpose([1, 0, 2])
                    dst = dfs[:, (gq % 2) * 2048:(gq % 2 + 1) * 2048].rearrange(
                        "p (a c) -> p a c", a=8)
                    inc(("ld", "ld2")[gq % 2], s.dma_start(dst, src), 16)

        @block.vector
        def _(v):
            for og in range(NDG):
                for ss in range(NT):
                    gq = og * 8 + ss
                    v.wait_ge(SEM[("ld", "ld2")[gq % 2]],
                              d_l[("ld", "ld2")[gq % 2]] + 16 * (gq // 2 + 1))
                    if og >= 1 and ss == 0:
                        v.wait_ge(SEM["mm"], d_mm0 + og * 8)
                    bsl = dfs[:, (gq % 2) * 2048:(gq % 2 + 1) * 2048]
                    qsl = wqd[:, ss * 2048:(ss + 1) * 2048]
                    v.tensor_scalar(qsl, bsl, thb[:, 2:3], None, ALU.is_gt)
                    v.tensor_scalar(dtn[:], bsl, nthb[:, 2:3], None, ALU.is_lt)
                    inc("q", v.tensor_tensor(qsl, qsl, dtn[:], ALU.subtract))
                v.wait_ge(SEM["ld3"], d_l["ld3"] + 16 * (og + 1))
                for j in range(NT):
                    m = og * 8 + j
                    v.wait_ge(SEM["mm"], d_mm0 + m + 1)
                    if og >= 1:
                        v.wait_ge(SEM["gp"], d_gp0 + og * 16)
                    ins = v.scalar_tensor_tensor(
                        bst[:, j * 256:(j + 1) * 256],
                        ps_d[:, (j % 2) * 512:(j % 2) * 512 + 256],
                        rqy[:, j:j + 1],
                        a1s[:, j * 256:(j + 1) * 256],
                        ALU.mult, ALU.add)
                    inc("stt", ins)

        @block.tensor
        def _(t):
            for og in range(NDG):
                t.wait_ge(SEM["q"], d_q0 + og * 8 + 8)
                for j in range(NT):
                    m = og * 8 + j
                    if m >= 2:
                        t.wait_ge(SEM["stt"], d_stt0 + m - 1)
                    for ic in range(OC):
                        ins = t.matmul(
                            ps_d[:, (j % 2) * 512:(j % 2) * 512 + 256],
                            yqT[:, j * 8192 + ic * 128: j * 8192 + (ic + 1) * 128],
                            wqd[:, ic * 256:(ic + 1) * 256],
                            start=(ic == 0), stop=(ic == OC - 1))
                    ins.then_inc(SEM["mm"], 1)

        @block.gpsimd
        def _(g):
            for og in range(NDG):
                g.wait_ge(SEM["stt"], d_stt0 + og * 8 + 8)
                inc("gp", g.dma_start(
                    b_scr.rearrange("(j p) c -> p j c", p=128)[
                        :, :, og * 256:(og + 1) * 256],
                    bst[:].rearrange("p (j c) -> p j c", j=NT)), 16)
            g.wait_ge(SEM["gp"], CNT["gp"])
    CNT["q"] = d_q0 + 64
    CNT["mm"] = d_mm0 + 64
    CNT["stt"] = d_stt0 + 64
    es4.close()
    es3a.close()

    es5 = ExitStack()
    lnt = sb(es5, "lnt", [128, 2 * D], F32)
    lns = sb(es5, "lns", [128, D], F32)
    l_ld0, l_ld20 = CNT["ld"], CNT["ld2"]
    l_d0, l_a0, l_st0 = CNT["da"], CNT["aa"], CNT["st"]
    with nc.Block() as block:
        @block.sync
        def _(s):
            for t in range(NT // 2):
                if t >= 1:
                    s.wait_ge(SEM["st"], l_st0 + 32 * t)
                inc("ld", s.dma_start(lnt[:, 0:D],
                                      b_scr[(2 * t) * 128:(2 * t + 1) * 128, :]), 16)
                inc("ld2", s.dma_start(lnt[:, D:2 * D],
                                       b_scr[(2 * t + 1) * 128:(2 * t + 2) * 128, :]), 16)
                s.wait_ge(SEM["da"], l_d0 + 4 * t + 4)
                inc("st", s.dma_start(out[(2 * t) * 128:(2 * t + 1) * 128, :],
                                      lnt[:, 0:D]), 16)
                inc("st", s.dma_start(out[(2 * t + 1) * 128:(2 * t + 2) * 128, :],
                                      lnt[:, D:2 * D]), 16)
            s.wait_ge(SEM["st"], CNT["st"])

        @block.vector
        def _(v):
            for t in range(NT // 2):
                t0 = lnt[:, 0:D]
                t1 = lnt[:, D:2 * D]
                v.wait_ge(SEM["ld"], l_ld0 + 16 * (t + 1))
                v.reduce_sum(tcol[:, 0:1], t0, AX.X)
                v.wait_ge(SEM["ld2"], l_ld20 + 16 * (t + 1))
                v.reduce_sum(tcol[:, 1:2], t1, AX.X)
                v.tensor_scalar(tcol[:, 0:1], tcol[:, 0:1], INV_D, None, ALU.mult)
                v.tensor_scalar(tcol[:, 1:2], tcol[:, 1:2], INV_D, None, ALU.mult)
                sep(v)
                inc("da", v.tensor_scalar(t0, t0, tcol[:, 0:1], None, ALU.subtract))
                inc("da", v.tensor_scalar(t1, t1, tcol[:, 1:2], None, ALU.subtract))
                v.wait_ge(SEM["aa"], l_a0 + 2 * (t + 1))
                v.reciprocal(tcol[:, 2:3], tcol[:, 2:3])
                v.reciprocal(tcol[:, 3:4], tcol[:, 3:4])
                sep(v)
                inc("da", v.tensor_scalar(t0, t0, tcol[:, 2:3], None, ALU.mult))
                inc("da", v.tensor_scalar(t1, t1, tcol[:, 3:4], None, ALU.mult))

        @block.scalar
        def _(sc):
            for t in range(NT // 2):
                t0 = lnt[:, 0:D]
                t1 = lnt[:, D:2 * D]
                sc.wait_ge(SEM["da"], l_d0 + 4 * t + 1)
                sc.activation(lns[:], t0, AFT.Square, bias=0.0, scale=1.0,
                              accum_out=tcol[:, 2:3])
                sc.wait_ge(SEM["da"], l_d0 + 4 * t + 2)
                sc.activation(lns[:], t1, AFT.Square, bias=0.0, scale=1.0,
                              accum_out=tcol[:, 3:4])
                ins = sc.activation(tcol[:, 2:3], tcol[:, 2:3], AFT.Sqrt,
                                    bias=EPS, scale=INV_D)
                ins.then_inc(SEM["aa"], 1)
                ins = sc.activation(tcol[:, 3:4], tcol[:, 3:4], AFT.Sqrt,
                                    bias=EPS, scale=INV_D)
                ins.then_inc(SEM["aa"], 1)
                sc.wait_ge(SEM["da"], l_d0 + 4 * t + 4)
        CNT["aa"] += NT
    es5.close()
    es0.close()
    return nc


def _host_layouts(x, w_gate, w_up, w_down):
    B, N, Dm = x.shape
    xf = np.ascontiguousarray(x.reshape(B * N, Dm))
    idx = np.arange(B * N).reshape(-1, 512)
    deint = np.concatenate([idx[:, 0::2], idx[:, 1::2]], axis=1).reshape(-1)
    xd_full = np.ascontiguousarray(xf[deint])

    wgt = np.ascontiguousarray(
        w_gate.T.reshape(HC, 128, NOG, 512).transpose(2, 0, 1, 3))
    wut = np.ascontiguousarray(
        w_up.T.reshape(HC, 128, NOG, 512).transpose(2, 0, 1, 3))
    wdt = np.ascontiguousarray(
        w_down.T.reshape(OC, 128, NDG, 256).transpose(2, 0, 1, 3))

    h = np.array([[1.0]], dtype=np.float64)
    for _ in range(7):
        h = np.block([[h, h], [h, -h]])
    h128 = h.astype(ml_dtypes.bfloat16)
    idn = np.eye(128, dtype=ml_dtypes.bfloat16)
    idnf = np.eye(128, dtype=np.float32)
    ones_r = np.ones((1, 128), np.float32)
    ones_c = np.ones((128, 1), np.float32)
    return xd_full, deint, wgt, wut, wdt, h128, idn, idnf, ones_r, ones_c


def kernel(L0_attn_x, L0_ffn_w_gate, L0_ffn_w_up, L0_ffn_w_down):
    x = np.asarray(L0_attn_x)
    B, N, Dm = x.shape
    xd_full, deint, wgt, wut, wdt, h128, idn, idnf, ones_r, ones_c = \
        _host_layouts(x, np.asarray(L0_ffn_w_gate), np.asarray(L0_ffn_w_up),
                      np.asarray(L0_ffn_w_down))

    nc = build_program(_STAGE)
    in_maps = []
    for c in range(8):
        xd_c = np.ascontiguousarray(xd_full[c * T:(c + 1) * T])
        in_maps.append(dict(
            xd=xd_c, xtr=np.ascontiguousarray(xd_c.T),
            wgt=wgt, wut=wut, wdt=wdt, h128=h128, idn=idn, idnf=idnf,
            ones_r=ones_r, ones_c=ones_c))
    res = run_bass_kernel_spmd(nc, in_maps, core_ids=list(range(8)))
    of = np.concatenate([r["out"] for r in res.results], axis=0)
    inv = np.empty_like(deint)
    inv[deint] = np.arange(B * N)
    return np.ascontiguousarray(of[inv].reshape(B, N, Dm))


# version 11 (final)
# speedup vs baseline: 1.0277x; 1.0277x over previous
from contextlib import ExitStack

import numpy as np
import ml_dtypes
import concourse.bass as bass
import concourse.mybir as mybir
from concourse.bass_utils import run_bass_kernel_spmd

F32 = mybir.dt.float32
BF16 = mybir.dt.bfloat16
FP16 = mybir.dt.float16
ALU = mybir.AluOpType
AFT = mybir.ActivationFunctionType
AX = mybir.AxisListType

T = 1024
D = 2048
HC = 16
OC = 64
NT = 8
NOG = 16
NDG = 8

C_ATT = float(np.float32(128.0) ** np.float32(-0.5))
B_ATT = float(np.float32(np.float32(128.0) ** np.float32(-0.5)) * np.float32(-176.0))
C_FWHT = float(np.float32(1.0 / np.sqrt(8192.0)))
EPS = float(np.float32(1e-5))
INV_D = float(np.float32(1.0 / 2048.0))
INV_NW = float(np.float32(1.0 / 16777216.0))
INV_128 = float(np.float32(1.0 / 128.0))

_STAGE = "full"


def _ladder(v, w0sl, valsl, btsl, outsl):
    for k in range(8, -1, -1):
        v.tensor_scalar(btsl, valsl, float(2 ** k), float(2 ** k), ALU.is_ge, ALU.mult)
        v.tensor_tensor(valsl, valsl, btsl, ALU.subtract)
    v.scalar_tensor_tensor(valsl, w0sl, -128.0, valsl, ALU.add, ALU.subtract)
    v.tensor_scalar(outsl, valsl, -128.0, 127.0, ALU.max, ALU.min)


def build_program(stage="full"):
    nc = bass.Bass("TRN2", target_bir_lowering=False,
                   detect_race_conditions=False)

    xd = nc.dram_tensor("xd", [T, D], F32, kind="ExternalInput").ap()
    xtr = nc.dram_tensor("xtr", [D, T], F32, kind="ExternalInput").ap()
    wgt = nc.dram_tensor("wgt", [NOG, HC, 128, 512], F32, kind="ExternalInput").ap()
    wut = nc.dram_tensor("wut", [NOG, HC, 128, 512], F32, kind="ExternalInput").ap()
    wdt = nc.dram_tensor("wdt", [NDG, OC, 128, 256], F32, kind="ExternalInput").ap()
    h128 = nc.dram_tensor("h128", [128, 128], BF16, kind="ExternalInput").ap()
    idn = nc.dram_tensor("idn", [128, 128], BF16, kind="ExternalInput").ap()
    idnf = nc.dram_tensor("idnf", [128, 128], F32, kind="ExternalInput").ap()
    ones_r = nc.dram_tensor("ones_r", [1, 128], F32, kind="ExternalInput").ap()
    ones_c = nc.dram_tensor("ones_c", [128, 1], F32, kind="ExternalInput").ap()

    out = nc.dram_tensor("out", [T, D], F32, kind="ExternalOutput").ap()

    a1_scr = nc.dram_tensor("a1_scr", [T, D], F32, kind="Internal").ap()
    b_scr = nc.dram_tensor("b_scr", [T, D], F32, kind="Internal").ap()
    i_scr = nc.dram_tensor("i_scr", [128, OC, T], F32, kind="Internal").ap()

    es0 = ExitStack()

    def sb(stk, name, shape, dtype):
        return stk.enter_context(nc.sbuf_tensor(name, shape, dtype))

    def pst(stk, name, shape, dtype):
        return stk.enter_context(nc.psum_tensor(name, shape, dtype))

    cst = sb(es0, "cst", [128, 6], F32)
    h128t = sb(es0, "h128t", [128, 128], BF16)
    idnt = sb(es0, "idnt", [128, 128], BF16)
    idnft = sb(es0, "idnft", [128, 128], F32)
    onesr = sb(es0, "onesr", [1, 128], F32)
    onesc = sb(es0, "onesc", [128, 1], F32)
    asum = sb(es0, "asum", [128, 4], F32)
    thb = sb(es0, "thb", [128, 3], F32)
    nthb = sb(es0, "nthb", [128, 3], F32)
    qv = sb(es0, "qv", [1, 4], F32)
    qgb = sb(es0, "qgb", [128, 1], F32)
    cattb = sb(es0, "cattb", [128, 1], F32)
    qub = sb(es0, "qub", [128, 1], F32)
    qdb = sb(es0, "qdb", [128, 1], F32)
    tiny = sb(es0, "tiny", [1, 8], F32)
    srx = sb(es0, "srx", [128, 8], F32)
    sry = sb(es0, "sry", [128, 8], F32)
    rqy = sb(es0, "rqy", [128, 8], F32)
    gxc = sb(es0, "gxc", [128, 8], F32)
    guc = sb(es0, "guc", [128, 8], F32)
    tcol = sb(es0, "tcol", [128, 4], F32)
    junk = sb(es0, "junk", [128, 256], F32)
    mx8 = sb(es0, "mx8", [128, 8], F32)
    sxv = sb(es0, "sxv", [128, 8], F32)

    nc.const_aps.register(F32, 128.5, cst[:, 0:1])
    nc.const_aps.register(F32, EPS, cst[:, 1:2])
    nc.const_aps.register(F32, INV_D, cst[:, 3:4])
    nc.const_aps.register(F32, B_ATT, cst[:, 4:5])

    es_cat = ExitStack()
    ps_cat = pst(es_cat, "ps_cat", [128, 1], F32)
    SEM = {}
    CNT = {}
    for nm in ("ld", "ld2", "ld3", "ld4", "ld5", "st", "pa", "pb", "pc", "da",
               "db", "dc", "aa", "gp", "gp2", "q", "q2", "mm", "tr", "ev", "si",
               "si2", "stt", "st2", "ld6"):
        SEM[nm] = es0.enter_context(nc.semaphore("s_" + nm))
        CNT[nm] = 0

    def inc(nm, ins, k=1):
        ins.then_inc(SEM[nm], k)
        CNT[nm] += k
        return ins

    def sep(v):
        v.tensor_scalar(junk[:], junk[:], 1.0, None, ALU.mult)

    with nc.Block() as block:
        @block.vector
        def _(v):
            v.memset(cst[:, 0:1], 128.5)
            v.memset(cst[:, 1:2], EPS)
            v.memset(cst[:, 2:3], 128.0)
            v.memset(cst[:, 3:4], INV_D)
            v.memset(cst[:, 4:5], B_ATT)
            v.memset(asum[:, :], 0.0)
            v.memset(junk[:], 1.0)

        @block.sync
        def _(s):
            for src, dst in ((h128, h128t), (idn, idnt), (idnf, idnft),
                             (ones_r, onesr), (ones_c, onesc)):
                inc("ld", s.dma_start(dst[:], src), 16)
            s.wait_ge(SEM["ld"], CNT["ld"])

    with nc.Block() as block:
        @block.vector
        def _(v):
            v.tensor_copy(tiny[:, 7:8], cst[0:1, 2:3])

    with nc.Block() as block:
        @block.scalar
        def _(sc):
            sc.activation(tiny[:, 7:8], tiny[:, 7:8], AFT.Sqrt, bias=0.0, scale=1.0)

    with nc.Block() as block:
        @block.vector
        def _(v):
            v.reciprocal(tiny[:, 7:8], tiny[:, 7:8])

    with nc.Block() as block:
        @block.tensor
        def _(t):
            t.matmul(ps_cat[:], onesr[:], tiny[:, 7:8], start=True, stop=True)

    with nc.Block() as block:
        @block.vector
        def _(v):
            v.tensor_copy(cattb[:], ps_cat[:])
    es_cat.close()

    es_x = ExitStack()
    xqT = sb(es_x, "xqT", [128, HC * T], BF16)

    es1 = ExitStack()
    xdt = sb(es1, "xdt", [128, NT * D], F32)
    xtb = sb(es1, "xtb", [128, HC * T], BF16)
    xdb = sb(es1, "xdb", [128, NT * D], BF16)
    et = sb(es1, "et", [128, 2 * 256], BF16)
    ett = sb(es1, "ett", [128, 2 * 256], BF16)
    rsum = sb(es1, "rsum", [128, 4], F32)
    lsc = sb(es1, "lsc", [128, D], F32)
    mw = sb(es1, "mw", [128, 2 * 2048], F32)
    red = sb(es1, "red", [128, 384], F32)
    l16a = sb(es1, "l16a", [128, 2048], FP16)
    l16b = sb(es1, "l16b", [128, 2048], FP16)
    l16c = sb(es1, "l16c", [128, 2048], FP16)
    ps_s = pst(es1, "ps_s", [128, 2 * 256], F32)
    ps_tr = pst(es1, "ps_tr", [128, 4 * 128], BF16)
    ps_o = pst(es1, "ps_o", [128, 2 * 128], F32)

    mn_si0 = CNT["si"]
    mn_si20 = CNT["si2"]
    MQ = ("ld", "ld2", "ld3", "ld4")
    mn_l = {}
    with nc.Block() as block:
        @block.sync
        def _(s):
            for j in range(NT):
                inc("ld", s.dma_start(xdt[:, j * D:(j + 1) * D],
                                      xd[j * 128:(j + 1) * 128, :]), 16)
            s.wait_ge(SEM["ld"], CNT["ld"])
            for q in MQ:
                mn_l[q] = CNT[q]

        @block.gpsimd
        def _(g):
            for j in range(HC):
                inc("ld5", g.dma_start(xtb[:, j * T:(j + 1) * T],
                                       xtr[j * 128:(j + 1) * 128, :]), 16)
            for j in range(NT):
                inc("ld5", g.dma_start(xdb[:, j * D:(j + 1) * D],
                                       xd[j * 128:(j + 1) * 128, :]), 16)
            g.wait_ge(SEM["ld5"], CNT["ld5"])

    ablocks = [(sg, pr, hh) for sg in range(2) for pr in range(2) for hh in range(8)]
    pa0, pb0, pc0 = CNT["pa"], CNT["pb"], CNT["pc"]
    da0, db0, dc0 = CNT["da"], CNT["db"], CNT["dc"]
    aa0 = CNT["aa"]
    with nc.Block() as block:
        @block.sync
        def _(s):
            k = 0
            for w in (wgt, wut):
                for j in range(16):
                    for qq in range(8):
                        if k >= 4:
                            s.wait_ge(SEM["si"], mn_si0 + k - 3)
                        wsrc = w[j, 2 * qq:2 * qq + 2].transpose([1, 0, 2])
                        dst = mw[:, (k % 4) * 1024:(k % 4 + 1) * 1024].rearrange(
                            "p (a c) -> p a c", a=2)
                        inc(MQ[k % 4], s.dma_start(dst, wsrc), 16)
                        k += 1

        @block.tensor
        def _(t):
            for i, (sg, pr, hh) in enumerate(ablocks):
                head = 2 * hh + pr
                base = 512 * sg + 256 * pr
                gt = xtb[:, head * T + base: head * T + base + 256]
                if i > 0:
                    t.wait_ge(SEM["aa"], aa0 + i)
                for qc in range(2):
                    ins = t.matmul(ps_s[:, qc * 256:(qc + 1) * 256],
                                   gt[:, qc * 128:(qc + 1) * 128], gt,
                                   start=True, stop=True)
                ins.then_inc(SEM["pa"], 1)
                t.wait_ge(SEM["aa"], aa0 + i + 1)
                if i > 0:
                    t.wait_ge(SEM["db"], db0 + i)
                for kc in range(2):
                    for qc in range(2):
                        z = 2 * kc + qc
                        ins = t.transpose(ps_tr[:, z * 128:(z + 1) * 128],
                                          et[:, qc * 256 + kc * 128:
                                             qc * 256 + kc * 128 + 128], idnt[:])
                ins.then_inc(SEM["pb"], 1)
                t.wait_ge(SEM["db"], db0 + i + 1)
                if i > 0:
                    t.wait_ge(SEM["dc"], dc0 + i)
                tb4 = base // 128
                for qc in range(2):
                    for kc in range(2):
                        gk = xdb[:, (tb4 + kc) * D + head * 128:
                                 (tb4 + kc) * D + head * 128 + 128]
                        ins = t.matmul(ps_o[:, qc * 128:(qc + 1) * 128],
                                       ett[:, kc * 256 + qc * 128:
                                           kc * 256 + qc * 128 + 128],
                                       gk, start=(kc == 0), stop=(kc == 1))
                ins.then_inc(SEM["pc"], 1)

        @block.vector
        def _(v):
            for i in range(32):
                sg, pr, hh = ablocks[i]
                head = 2 * hh + pr
                base = 512 * sg + 256 * pr
                v.wait_ge(SEM["pb"], pb0 + i + 1)
                for kc in range(2):
                    for qc in range(2):
                        z = 2 * kc + qc
                        ins = v.tensor_copy(ett[:, kc * 256 + qc * 128:
                                                kc * 256 + qc * 128 + 128],
                                            ps_tr[:, z * 128:(z + 1) * 128])
                ins.then_inc(SEM["db"], 1)
                v.wait_ge(SEM["pc"], pc0 + i + 1)
                tb4 = base // 128
                for qc in range(2):
                    v.reciprocal(rsum[:, qc:qc + 1], rsum[:, qc:qc + 1])
                sep(v)
                for qc in range(2):
                    dst = xdt[:, (tb4 + qc) * D + head * 128:
                              (tb4 + qc) * D + head * 128 + 128]
                    ins = v.scalar_tensor_tensor(dst, ps_o[:, qc * 128:(qc + 1) * 128],
                                                 rsum[:, qc:qc + 1], dst,
                                                 ALU.mult, ALU.add)
                ins.then_inc(SEM["dc"], 1)

        @block.scalar
        def _(sc):
            kch = [0]

            def do_chunks(n):
                for _ in range(n):
                    k = kch[0]
                    sc.wait_ge(SEM[MQ[k % 4]],
                               mn_l[MQ[k % 4]] + 16 * (k // 4 + 1))
                    buf = mw[:, (k % 4) * 1024:(k % 4 + 1) * 1024]
                    ins = sc.activation(buf, buf, AFT.Abs, bias=0.0,
                                        scale=1.0, accum_out=red[:, k:k + 1])
                    ins.then_inc(SEM["si"], 1)
                    kch[0] += 1

            for i in range(32):
                sc.wait_ge(SEM["pa"], pa0 + i + 1)
                if i > 0:
                    sc.wait_ge(SEM["pb"], pb0 + i)
                for qc in range(2):
                    ins = sc.activation(et[:, qc * 256:(qc + 1) * 256],
                                        ps_s[:, qc * 256:(qc + 1) * 256],
                                        AFT.Exp, bias=B_ATT,
                                        scale=cattb[:, 0:1],
                                        accum_out=rsum[:, qc:qc + 1])
                ins.then_inc(SEM["aa"], 1)
                do_chunks(8)
            CNT["si"] = mn_si0 + 256
    CNT["pa"] = pa0 + 32
    CNT["pb"] = pb0 + 32
    CNT["pc"] = pc0 + 32
    CNT["db"] = db0 + 32
    CNT["dc"] = dc0 + 32
    CNT["aa"] = aa0 + 32

    with nc.Block() as block:
        @block.vector
        def _(v):
            for j in range(128):
                for mi in range(2):
                    v.tensor_tensor(asum[:, mi:mi + 1], asum[:, mi:mi + 1],
                                    red[:, mi * 128 + j:mi * 128 + j + 1],
                                    ALU.add)

    es_f = ExitStack()
    ps_fin = pst(es_f, "ps_fin", [128, 8], F32)
    f_pa0, f_da0, f_aa0 = CNT["pa"], CNT["da"], CNT["aa"]
    with nc.Block() as block:
        @block.tensor
        def _(t):
            for mi in range(2):
                ins = t.matmul(ps_fin[0:1, mi:mi + 1], asum[:, mi:mi + 1],
                               onesc[:], start=True, stop=True)
            ins.then_inc(SEM["pa"], 1)
            t.wait_ge(SEM["da"], f_da0 + 2)
            for mi in range(2):
                t.matmul(ps_fin[:, mi:mi + 1], onesr[:], tiny[:, mi:mi + 1],
                         start=True, stop=True)
            for z in range(2):
                ins = t.matmul(ps_fin[:, 3 + z:4 + z], onesr[:], qv[:, z:z + 1],
                               start=True, stop=True)
            ins.then_inc(SEM["pa"], 1)

        @block.vector
        def _(v):
            v.wait_ge(SEM["pa"], f_pa0 + 1)
            for mi in range(2):
                v.tensor_copy(tiny[:, mi:mi + 1], ps_fin[0:1, mi:mi + 1])
            sep(v)
            for mi in range(2):
                v.tensor_scalar(tiny[:, mi:mi + 1], tiny[:, mi:mi + 1],
                                INV_NW, EPS, ALU.mult, ALU.max)
            sep(v)
            for mi in range(2):
                v.reciprocal(tiny[:, 4 + mi:5 + mi], tiny[:, mi:mi + 1])
            sep(v)
            for mi in range(2):
                v.reciprocal(qv[:, mi:mi + 1], tiny[:, 4 + mi:5 + mi])
            sep(v)
            for mi in range(2):
                ins = v.tensor_scalar(tiny[:, mi:mi + 1], tiny[:, mi:mi + 1],
                                      0.5, None, ALU.mult)
            ins.then_inc(SEM["da"], 1)
            ins = v.tensor_scalar(junk[:, 0:1], junk[:, 0:1], 1.0, None, ALU.add)
            ins.then_inc(SEM["da"], 1)
            v.wait_ge(SEM["pa"], f_pa0 + 2)
            for mi in range(2):
                v.tensor_copy(thb[:, mi:mi + 1], ps_fin[:, mi:mi + 1])
                v.tensor_scalar(nthb[:, mi:mi + 1], ps_fin[:, mi:mi + 1],
                                -1.0, None, ALU.mult)
            v.tensor_copy(qgb[:], ps_fin[:, 3:4])
            v.tensor_copy(qub[:], ps_fin[:, 4:5])

    CNT["pa"] = f_pa0 + 2
    CNT["da"] = f_da0 + 2
    es_f.close()


    if stage == "attn":
        with nc.Block() as block:
            @block.sync
            def _(s):
                for j in range(NT):
                    inc("st", s.dma_start(out[j * 128:(j + 1) * 128, :],
                                          xdt[:, j * D:(j + 1) * D]), 16)
                s.wait_ge(SEM["st"], CNT["st"])
        es1.close(); es_x.close(); es0.close()
        return nc

    ln_d0, ln_a0 = CNT["da"], CNT["aa"]
    with nc.Block() as block:
        @block.vector
        def _(v):
            for t in range(NT // 2):
                t0 = xdt[:, (2 * t) * D:(2 * t + 1) * D]
                t1 = xdt[:, (2 * t + 1) * D:(2 * t + 2) * D]
                v.reduce_sum(tcol[:, 0:1], t0, AX.X)
                v.reduce_sum(tcol[:, 1:2], t1, AX.X)
                v.tensor_scalar(tcol[:, 0:1], tcol[:, 0:1], INV_D, None, ALU.mult)
                v.tensor_scalar(tcol[:, 1:2], tcol[:, 1:2], INV_D, None, ALU.mult)
                sep(v)
                inc("da", v.tensor_scalar(t0, t0, tcol[:, 0:1], None, ALU.subtract))
                inc("da", v.tensor_scalar(t1, t1, tcol[:, 1:2], None, ALU.subtract))
                v.wait_ge(SEM["aa"], ln_a0 + 2 * (t + 1))
                v.reciprocal(tcol[:, 2:3], tcol[:, 2:3])
                v.reciprocal(tcol[:, 3:4], tcol[:, 3:4])
                sep(v)
                inc("da", v.tensor_scalar(t0, t0, tcol[:, 2:3], None, ALU.mult))
                inc("da", v.tensor_scalar(t1, t1, tcol[:, 3:4], None, ALU.mult))

        @block.scalar
        def _(sc):
            for t in range(NT // 2):
                t0 = xdt[:, (2 * t) * D:(2 * t + 1) * D]
                t1 = xdt[:, (2 * t + 1) * D:(2 * t + 2) * D]
                sc.wait_ge(SEM["da"], ln_d0 + 4 * t + 1)
                sc.activation(lsc[:], t0, AFT.Square, bias=0.0, scale=1.0,
                              accum_out=tcol[:, 2:3])
                sc.wait_ge(SEM["da"], ln_d0 + 4 * t + 2)
                sc.activation(lsc[:], t1, AFT.Square, bias=0.0, scale=1.0,
                              accum_out=tcol[:, 3:4])
                ins = sc.activation(tcol[:, 2:3], tcol[:, 2:3], AFT.Sqrt,
                                    bias=EPS, scale=INV_D)
                ins.then_inc(SEM["aa"], 1)
                ins = sc.activation(tcol[:, 3:4], tcol[:, 3:4], AFT.Sqrt,
                                    bias=EPS, scale=INV_D)
                ins.then_inc(SEM["aa"], 1)
                sc.wait_ge(SEM["da"], ln_d0 + 4 * t + 4)
        CNT["aa"] += NT

    with nc.Block() as block:
        @block.gpsimd
        def _(g):
            for j in range(NT):
                inc("st2", g.dma_start(a1_scr[j * 128:(j + 1) * 128, :],
                                       xdt[:, j * D:(j + 1) * D]), 16)
            g.wait_ge(SEM["st2"], CNT["st2"])

    if stage == "a1":
        with nc.Block() as block:
            @block.sync
            def _(s):
                for j in range(NT):
                    inc("st", s.dma_start(out[j * 128:(j + 1) * 128, :],
                                          xdt[:, j * D:(j + 1) * D]), 16)
                s.wait_ge(SEM["st"], CNT["st"])
        es1.close(); es_x.close(); es0.close()
        return nc

    xq_d0, xq_a0 = CNT["da"], CNT["aa"]
    with nc.Block() as block:
        @block.vector
        def _(v):
            for j in range(NT):
                v.reduce_max(mx8[:, j:j + 1], xdt[:, j * D:(j + 1) * D], AX.X,
                             apply_absolute_value=True)
            for j in range(NT):
                v.tensor_scalar(srx[:, j:j + 1], mx8[:, j:j + 1], EPS, INV_128,
                                ALU.max, ALU.mult)
            for j in range(NT):
                v.reciprocal(sxv[:, j:j + 1], srx[:, j:j + 1])
            for j in range(NT):
                v.tensor_tensor(gxc[:, j:j + 1], srx[:, j:j + 1], qgb[:], ALU.mult)
            for j in range(NT):
                ins = v.tensor_tensor(guc[:, j:j + 1], srx[:, j:j + 1], qub[:],
                                      ALU.mult)
            inc("da", ins)
            for j in range(NT):
                tile = xdt[:, j * D:(j + 1) * D]
                v.tensor_scalar(l16a[:], tile, sxv[:, j:j + 1], 128.5,
                                ALU.mult, ALU.add)
                v.tensor_scalar(l16b[:], tile, sxv[:, j:j + 1], 128.5,
                                ALU.mult, ALU.add)
                _ladder(v, l16a[:], l16b[:], l16c[:], xdb[:, j * D:(j + 1) * D])

    tr_p0, tr_d0 = CNT["pa"], CNT["da"]
    es_tp = ExitStack()
    ps_x = pst(es_tp, "ps_x", [128, 2 * 1024], BF16)
    with nc.Block() as block:
        @block.tensor
        def _(t):
            g = 0
            for j in range(NT):
                for hcg in range(4):
                    if g >= 2:
                        t.wait_ge(SEM["da"], tr_d0 + g - 1)
                    for z in range(4):
                        hc = hcg * 4 + z
                        ins = t.transpose(
                            ps_x[:, (g % 2) * 1024 + z * 128:
                                 (g % 2) * 1024 + (z + 1) * 128],
                            xdb[:, j * D + hc * 128: j * D + hc * 128 + 128], idnt[:])
                    ins.then_inc(SEM["pa"], 1)
                    g += 1

        @block.vector
        def _(v):
            g = 0
            for j in range(NT):
                for hcg in range(4):
                    v.wait_ge(SEM["pa"], tr_p0 + g + 1)
                    for z in range(4):
                        hc = hcg * 4 + z
                        ins = v.tensor_copy(
                            xqT[:, hc * T + j * 128: hc * T + (j + 1) * 128],
                            ps_x[:, (g % 2) * 1024 + z * 128:
                                 (g % 2) * 1024 + (z + 1) * 128])
                    ins.then_inc(SEM["da"], 1)
                    g += 1
    CNT["pa"] = tr_p0 + 32
    CNT["da"] = tr_d0 + 32
    es_tp.close()

    if stage == "xq":
        with nc.Block() as block:
            @block.vector
            def _(v):
                v.tensor_copy(lsc[:, 0:1024], xqT[:, 0:T])
        with nc.Block() as block:
            @block.sync
            def _(s):
                inc("st", s.dma_start(out[0:128, 0:1024], lsc[:, 0:1024]), 16)
                s.wait_ge(SEM["st"], CNT["st"])
        es1.close(); es_x.close(); es0.close()
        return nc

    es1.close()

    es2 = ExitStack()
    wfg = sb(es2, "wfg", [128, 2 * 2048], F32)
    wfu = sb(es2, "wfu", [128, 2 * 2048], F32)
    wqg = sb(es2, "wqg", [128, 2 * 8192], BF16)
    wqu = sb(es2, "wqu", [128, 2 * 8192], BF16)
    tpb = sb(es2, "tpb", [128, 2048], BF16)
    tnb = sb(es2, "tnb", [128, 2048], BF16)
    sgt = sb(es2, "sgt", [128, 2 * T], F32)
    ist = sb(es2, "ist", [128, 2 * 4 * T], F32)
    grwx = sb(es2, "grwx", [1, T], F32)
    gxB = sb(es2, "gxB", [128, T], F32)
    mwd = sb(es2, "mwd", [128, 4 * 1024], F32)
    red2 = sb(es2, "red2", [128, 128], F32)
    ps_g = pst(es2, "ps_g", [128, 2 * T], F32)
    ps_u = pst(es2, "ps_u", [128, 2 * T], F32)

    gx_p0, gx_d0 = CNT["pa"], CNT["da"]
    with nc.Block() as block:
        @block.tensor
        def _(t):
            for j in range(NT):
                ins = t.transpose(ps_g[0:1, j * 128:(j + 1) * 128],
                                  srx[:, j:j + 1], idnft[:])
            ins.then_inc(SEM["pa"], 1)
            t.wait_ge(SEM["da"], gx_d0 + 1)
            for hf in range(2):
                ins = t.matmul(ps_u[:, hf * 512:(hf + 1) * 512], onesr[:],
                               grwx[0:1, hf * 512:(hf + 1) * 512],
                               start=True, stop=True)
            ins.then_inc(SEM["pa"], 1)

        @block.vector
        def _(v):
            v.wait_ge(SEM["pa"], gx_p0 + 1)
            for j in range(NT):
                v.tensor_copy(grwx[0:1, j * 128:(j + 1) * 128],
                              ps_g[0:1, j * 128:(j + 1) * 128])
            sep(v)
            ins = v.tensor_scalar(grwx[0:1, :], grwx[0:1, :], qv[:, 0:1],
                                  None, ALU.mult)
            ins.then_inc(SEM["da"], 1)
            v.wait_ge(SEM["pa"], gx_p0 + 2)
            v.tensor_copy(gxB[:, 0:512], ps_u[:, 0:512])
            v.tensor_copy(gxB[:, 512:1024], ps_u[:, 512:1024])
    CNT["pa"] = gx_p0 + 2
    CNT["da"] = gx_d0 + 1

    g_l = {nm: CNT[nm] for nm in ("ld", "ld2", "ld3", "ld4")}
    d_l60, d_si20 = CNT["ld6"], CNT["si2"]
    g_q0, g_q20 = CNT["q"], CNT["q2"]
    g_mm0, g_si0, g_ev0 = CNT["mm"], CNT["si"], CNT["ev"]
    g_gp0, g_gp20 = CNT["gp"], CNT["gp2"]
    g_gr0 = CNT["dc"]
    with nc.Block() as block:
        @block.sync
        def _(s):
            for og in range(NOG):
                for qq in range(4):
                    gq = og * 4 + qq
                    if gq >= 2:
                        s.wait_ge(SEM["q"], g_q0 + gq - 1)
                    src = wgt[og, 4 * qq:4 * qq + 4].transpose([1, 0, 2])
                    dst = wfg[:, (gq % 2) * 2048:(gq % 2 + 1) * 2048].rearrange(
                        "p (a c) -> p a c", a=4)
                    inc(("ld", "ld2")[qq % 2], s.dma_start(dst, src), 16)
                for qq in range(4):
                    gq = og * 4 + qq
                    if gq >= 2:
                        s.wait_ge(SEM["q2"], g_q20 + gq - 1)
                    src = wut[og, 4 * qq:4 * qq + 4].transpose([1, 0, 2])
                    dst = wfu[:, (gq % 2) * 2048:(gq % 2 + 1) * 2048].rearrange(
                        "p (a c) -> p a c", a=4)
                    inc(("ld3", "ld4")[qq % 2], s.dma_start(dst, src), 16)

        @block.vector
        def _(v):
            for og in range(NOG + 1):
                if og < NOG:
                    if og >= 2:
                        v.wait_ge(SEM["mm"], g_mm0 + (og - 1) * 4)
                    for half in range(2):
                        qn = "q" if half == 0 else "q2"
                        buf = wfg if half == 0 else wfu
                        wq = wqg if half == 0 else wqu
                        for qq in range(4):
                            gq = og * 4 + qq
                            lnm = (("ld", "ld2") if half == 0 else ("ld3", "ld4"))[qq % 2]
                            v.wait_ge(SEM[lnm], g_l[lnm] + 16 * (og * 2 + qq // 2 + 1))
                            bsl = buf[:, (gq % 2) * 2048:(gq % 2 + 1) * 2048]
                            qsl = wq[:, (og % 2) * 8192 + qq * 2048:
                                     (og % 2) * 8192 + (qq + 1) * 2048]
                            v.tensor_scalar(tpb[:], bsl, thb[:, half:half + 1],
                                            None, ALU.is_gt)
                            v.tensor_scalar(tnb[:], bsl, nthb[:, half:half + 1],
                                            None, ALU.is_lt)
                            inc(qn, v.tensor_tensor(qsl, tpb[:], tnb[:], ALU.subtract))
                if og >= 1:
                    po = og - 1
                    for oc4 in range(4):
                        m = po * 4 + oc4
                        p = m % 2
                        v.wait_ge(SEM["mm"], g_mm0 + m + 1)
                        ins = v.tensor_tensor(sgt[:, p * T:p * T + T],
                                              ps_g[:, p * T:p * T + T],
                                              gxB[:], ALU.mult)
                        inc("dc", ins)
                        v.wait_ge(SEM["si"], g_si0 + m + 1)
                        if po >= 2:
                            v.wait_ge(SEM[("gp", "gp2")[po % 2]],
                                      (g_gp0, g_gp20)[po % 2] + 64 * ((po - 2) // 2 + 1))
                        ins = v.tensor_tensor(
                            ist[:, (po % 2) * 4 * T + oc4 * T:
                                (po % 2) * 4 * T + (oc4 + 1) * T],
                            sgt[:, p * T:p * T + T],
                            ps_u[:, p * T:p * T + T], ALU.mult)
                        inc("ev", ins)

        @block.tensor
        def _(t):
            for og in range(NOG):
                t.wait_ge(SEM["q"], g_q0 + og * 4 + 4)
                t.wait_ge(SEM["q2"], g_q20 + og * 4 + 4)
                for oc4 in range(4):
                    m = og * 4 + oc4
                    p = m % 2
                    if m >= 2:
                        t.wait_ge(SEM["ev"], g_ev0 + m - 1)
                    for half in range(2):
                        wq = wqg if half == 0 else wqu
                        pdst = ps_g if half == 0 else ps_u
                        for hc in range(HC):
                            lh = wq[:, (og % 2) * 8192 + hc * 512 + oc4 * 128:
                                    (og % 2) * 8192 + hc * 512 + oc4 * 128 + 128]
                            ins = t.matmul(pdst[:, p * T:p * T + 512], lh,
                                           xqT[:, hc * T:hc * T + 512],
                                           start=(hc == 0), stop=(hc == HC - 1))
                            ins = t.matmul(pdst[:, p * T + 512:p * T + 1024], lh,
                                           xqT[:, hc * T + 512:hc * T + 1024],
                                           start=(hc == 0), stop=(hc == HC - 1))
                    ins.then_inc(SEM["mm"], 1)

        @block.scalar
        def _(sc):
            for og in range(NOG):
                for oc4 in range(4):
                    m = og * 4 + oc4
                    p = m % 2
                    sc.wait_ge(SEM["dc"], g_gr0 + m + 1)
                    ins = sc.activation(sgt[:, p * T:p * T + T],
                                        sgt[:, p * T:p * T + T],
                                        AFT.Silu, bias=0.0, scale=1.0)
                    ins.then_inc(SEM["si"], 1)
                    for kc in (2 * m, 2 * m + 1):
                        sc.wait_ge(SEM["ld6"], d_l60 + 16 * (kc + 1))
                        buf = mwd[:, (kc % 4) * 1024:(kc % 4 + 1) * 1024]
                        ins = sc.activation(buf, buf, AFT.Abs, bias=0.0,
                                            scale=1.0,
                                            accum_out=red2[:, kc:kc + 1])
                        ins.then_inc(SEM["si2"], 1)
            CNT["si2"] = d_si20 + 128

        @block.gpsimd
        def _(g):
            for og in range(NOG):
                for c in range(8):
                    kc = og * 8 + c
                    if kc >= 4:
                        g.wait_ge(SEM["si2"], d_si20 + kc - 3)
                    j, qq = kc // 8, kc % 8
                    src = wdt[j // 2, (j % 2) * 32 + 4 * qq:
                              (j % 2) * 32 + 4 * qq + 4].transpose([1, 0, 2])
                    dst = mwd[:, (kc % 4) * 1024:
                              (kc % 4 + 1) * 1024].rearrange(
                        "p (a c) -> p a c", a=4)
                    inc("ld6", g.dma_start(dst, src), 16)
                g.wait_ge(SEM["ev"], g_ev0 + og * 4 + 4)
                for z in range(4):
                    inc(("gp", "gp2")[og % 2], g.dma_start(
                        i_scr[:, og * 4 + z:og * 4 + z + 1, :].squeeze(1),
                        ist[:, (og % 2) * 4 * T + z * T:
                            (og % 2) * 4 * T + (z + 1) * T]), 16)
            g.wait_ge(SEM["gp"], CNT["gp"])
            g.wait_ge(SEM["gp2"], CNT["gp2"])
    CNT["q"] = g_q0 + 64
    CNT["q2"] = g_q20 + 64
    CNT["mm"] = g_mm0 + 64
    CNT["si"] = g_si0 + 64
    CNT["dc"] = g_gr0 + 64
    CNT["ev"] = g_ev0 + 64

    with nc.Block() as block:
        @block.vector
        def _(v):
            for j in range(64):
                v.tensor_tensor(asum[:, 2:3], asum[:, 2:3],
                                red2[:, 2 * j:2 * j + 1], ALU.add)
                v.tensor_tensor(asum[:, 3:4], asum[:, 3:4],
                                red2[:, 2 * j + 1:2 * j + 2], ALU.add)
            sep(v)
            v.tensor_tensor(asum[:, 2:3], asum[:, 2:3], asum[:, 3:4], ALU.add)

    fb_pa0, fb_da0 = CNT["pa"], CNT["da"]
    with nc.Block() as block:
        @block.tensor
        def _(t):
            ins = t.matmul(ps_g[0:1, 0:1], asum[:, 2:3], onesc[:],
                           start=True, stop=True)
            ins.then_inc(SEM["pa"], 1)
            t.wait_ge(SEM["da"], fb_da0 + 2)
            t.matmul(ps_g[:, 0:1], onesr[:], tiny[:, 2:3],
                     start=True, stop=True)
            ins = t.matmul(ps_g[:, 1:2], onesr[:], qv[:, 2:3],
                           start=True, stop=True)
            ins.then_inc(SEM["pa"], 1)

        @block.vector
        def _(v):
            v.wait_ge(SEM["pa"], fb_pa0 + 1)
            v.tensor_copy(tiny[:, 2:3], ps_g[0:1, 0:1])
            sep(v)
            v.tensor_scalar(tiny[:, 2:3], tiny[:, 2:3],
                            INV_NW, EPS, ALU.mult, ALU.max)
            sep(v)
            v.reciprocal(tiny[:, 6:7], tiny[:, 2:3])
            sep(v)
            v.reciprocal(qv[:, 2:3], tiny[:, 6:7])
            sep(v)
            ins = v.tensor_scalar(tiny[:, 2:3], tiny[:, 2:3],
                                  0.5, None, ALU.mult)
            ins.then_inc(SEM["da"], 1)
            ins = v.tensor_scalar(junk[:, 0:1], junk[:, 0:1], 1.0, None, ALU.add)
            ins.then_inc(SEM["da"], 1)
            v.wait_ge(SEM["pa"], fb_pa0 + 2)
            v.tensor_copy(thb[:, 2:3], ps_g[:, 0:1])
            v.tensor_scalar(nthb[:, 2:3], ps_g[:, 0:1], -1.0, None, ALU.mult)
            v.tensor_copy(qdb[:], ps_g[:, 1:2])
    CNT["pa"] = fb_pa0 + 2
    CNT["da"] = fb_da0 + 2

    es2.close()
    es_x.close()

    es3a = ExitStack()
    yqT = sb(es3a, "yqT", [128, OC * T], BF16)
    es3 = ExitStack()
    ibuf = sb(es3, "ibuf", [128, OC * 128], F32)
    l16a = sb(es3, "m16a", [128, 2048], FP16)
    l16b = sb(es3, "m16b", [128, 2048], FP16)
    l16c = sb(es3, "m16c", [128, 2048], FP16)
    hib = sb(es3, "hib", [128, 2 * 512], BF16)
    lob = sb(es3, "lob", [128, 2 * 512], BF16)
    mby = sb(es3, "mby", [128, 128], F32)
    csr = sb(es3, "csr", [1, 128], F32)
    scb = sb(es3, "scb", [128, 128], F32)
    ps_z = pst(es3, "ps_z", [128, 2 * 512], F32)
    ps_m = pst(es3, "ps_m", [128, 128], F32)
    ps_r = pst(es3, "ps_r", [1, 128], F32)
    ps_b = pst(es3, "ps_b", [128, 128], F32)

    fw_ld0 = CNT["ld"]
    fw_ev0 = CNT["ev"]
    for tb in range(NT):
        if stage == "fw1" and tb == 1:
            with nc.Block() as block:
                @block.sync
                def _(s):
                    for ch in range(4):
                        inc("st", s.dma_start(out[ch * 128:(ch + 1) * 128, 0:2048],
                                              ibuf[:, ch * 2048:(ch + 1) * 2048]), 16)
                    s.wait_ge(SEM["st"], CNT["st"])
            es3.close(); es3a.close(); es0.close()
            return nc

        f_q0, f_p0, f_e0 = CNT["q"], CNT["pa"], CNT["db"]
        f_pb0, f_dc0 = CNT["pb"], CNT["dc"]
        with nc.Block() as block:
            @block.sync
            def _(s):
                if tb >= 1:
                    s.wait_ge(SEM["ev"], fw_ev0 + tb)
                inc("ld", s.dma_start(
                    ibuf[:].rearrange("p (a t) -> p a t", a=OC),
                    i_scr[:, :, tb * 128:(tb + 1) * 128]), 16)

            @block.vector
            def _(v):
                v.wait_ge(SEM["ld"], CNT["ld"])
                for st_ in range(6):
                    ngrp = OC // (2 ** (st_ + 1))
                    vw = ibuf[:].rearrange("p (g two l) -> p g two l",
                                           g=ngrp, two=2)
                    lo = vw[:, :, 0, :]
                    hi = vw[:, :, 1, :]
                    v.tensor_tensor(lo, lo, hi, ALU.add)
                    v.scalar_tensor_tensor(hi, hi, -2.0, lo, ALU.mult, ALU.add)
                for gg in range(16):
                    if gg >= 2:
                        v.wait_ge(SEM["pa"], f_p0 + gg - 1)
                    sl = ibuf[:, gg * 512:(gg + 1) * 512]
                    hsl = hib[:, (gg % 2) * 512:(gg % 2 + 1) * 512]
                    lsl = lob[:, (gg % 2) * 512:(gg % 2 + 1) * 512]
                    v.tensor_copy(hsl, sl)
                    inc("q", v.tensor_tensor(lsl, sl, hsl, ALU.subtract))
                    if gg >= 1:
                        v.wait_ge(SEM["pa"], f_p0 + gg)
                        pg = gg - 1
                        inc("db", v.tensor_copy(
                            ibuf[:, pg * 512:(pg + 1) * 512],
                            ps_z[:, (pg % 2) * 512:(pg % 2 + 1) * 512]))
                v.wait_ge(SEM["pa"], f_p0 + 16)
                inc("db", v.tensor_copy(ibuf[:, 15 * 512:16 * 512],
                                        ps_z[:, 512:1024]))
                ins = v.tensor_reduce(mby[:], ibuf[:].rearrange(
                    "p (a t) -> p t a", a=OC), AX.X, ALU.max,
                    apply_absolute_value=True)
                inc("dc", ins)
                v.wait_ge(SEM["pb"], f_pb0 + 1)
                v.reduce_max(tcol[:, 0:1], ps_m[:], AX.X)
                sep(v)
                v.tensor_tensor(tcol[:, 0:1], tcol[:, 0:1], guc[:, tb:tb + 1],
                                ALU.mult)
                sep(v)
                v.tensor_scalar(tcol[:, 0:1], tcol[:, 0:1], C_FWHT, EPS,
                                ALU.mult, ALU.max)
                sep(v)
                v.tensor_scalar(sry[:, tb:tb + 1], tcol[:, 0:1], INV_128, None,
                                ALU.mult)
                sep(v)
                v.tensor_tensor(rqy[:, tb:tb + 1], sry[:, tb:tb + 1], qdb[:],
                                ALU.mult)
                sep(v)
                v.reciprocal(tcol[:, 1:2], sry[:, tb:tb + 1])
                sep(v)
                v.tensor_scalar(tcol[:, 1:2], tcol[:, 1:2], C_FWHT, None,
                                ALU.mult)
                sep(v)
                ins = v.tensor_tensor(tcol[:, 1:2], tcol[:, 1:2], guc[:, tb:tb + 1],
                                      ALU.mult)
                inc("dc", ins)
                v.wait_ge(SEM["pb"], f_pb0 + 2)
                inc("dc", v.tensor_copy(csr[:], ps_r[:]))
                v.wait_ge(SEM["pb"], f_pb0 + 3)
                v.tensor_copy(scb[:], ps_b[:])
                for z in range(4):
                    sl = ibuf[:, z * 2048:(z + 1) * 2048]
                    slv = sl.rearrange("p (a t) -> p a t", a=16)
                    bcast = scb[:].unsqueeze(1).broadcast_to([128, 16, 128])
                    v.tensor_tensor(slv, slv, bcast, ALU.mult)
                    v.tensor_scalar(l16a[:], sl, 128.5, None, ALU.add)
                    v.tensor_scalar(l16b[:], sl, 128.5, None, ALU.add)
                    yout = yqT[:, tb * 8192 + z * 2048: tb * 8192 + (z + 1) * 2048]
                    _ladder(v, l16a[:], l16b[:], l16c[:], yout)
                inc("ev", v.nop())

            @block.tensor
            def _(t):
                for gg in range(16):
                    t.wait_ge(SEM["q"], f_q0 + gg + 1)
                    if gg >= 2:
                        t.wait_ge(SEM["db"], f_e0 + gg - 1)
                    hsl = hib[:, (gg % 2) * 512:(gg % 2 + 1) * 512]
                    lsl = lob[:, (gg % 2) * 512:(gg % 2 + 1) * 512]
                    t.matmul(ps_z[:, (gg % 2) * 512:(gg % 2 + 1) * 512],
                             h128t[:], hsl, start=True, stop=False)
                    ins = t.matmul(ps_z[:, (gg % 2) * 512:(gg % 2 + 1) * 512],
                                   h128t[:], lsl, start=False, stop=True)
                    ins.then_inc(SEM["pa"], 1)
                t.wait_ge(SEM["dc"], f_dc0 + 1)
                ins = t.transpose(ps_m[:], mby[:], idnft[:])
                ins.then_inc(SEM["pb"], 1)
                t.wait_ge(SEM["dc"], f_dc0 + 2)
                ins = t.transpose(ps_r[:], tcol[:, 1:2], idnft[:])
                ins.then_inc(SEM["pb"], 1)
                t.wait_ge(SEM["dc"], f_dc0 + 3)
                ins = t.matmul(ps_b[:], onesr[:], csr[:], start=True, stop=True)
                ins.then_inc(SEM["pb"], 1)
        CNT["q"] = f_q0 + 16
        CNT["pa"] = f_p0 + 16
        CNT["db"] = f_e0 + 16
        CNT["pb"] = f_pb0 + 3
        CNT["dc"] = f_dc0 + 3
        CNT["ev"] = fw_ev0 + tb + 1

    if stage == "yq":
        with nc.Block() as block:
            @block.vector
            def _(v):
                v.tensor_copy(ibuf[:, 4096:6144], yqT[:, 0:2048])
                v.tensor_copy(ibuf[:, 0:2048], yqT[:, 2048:4096])
        with nc.Block() as block:
            @block.sync
            def _(s):
                inc("st", s.dma_start(out[0:128, 0:2048], ibuf[:, 4096:6144]), 16)
                inc("st", s.dma_start(out[128:256, 0:2048], ibuf[:, 0:2048]), 16)
                inc("st", s.dma_start(out[256:384, 0:8], sry[:, :]), 16)
                s.wait_ge(SEM["st"], CNT["st"])
        es3.close(); es3a.close(); es0.close()
        return nc

    es3.close()

    es4 = ExitStack()
    dfs = sb(es4, "dfs", [128, 2 * 2048], F32)
    wqd = sb(es4, "wqd", [128, OC * 256], BF16)
    dtn = sb(es4, "dtn", [128, 2048], BF16)
    a1s = sb(es4, "a1s", [128, NT * 256], F32)
    bst = sb(es4, "bst", [128, NT * 256], F32)
    ps_d = pst(es4, "ps_d", [128, 2 * 512], F32)

    d_l = {nm: CNT[nm] for nm in ("ld", "ld2", "ld3")}
    d_q0, d_mm0, d_stt0, d_gp0 = (CNT["q"], CNT["mm"], CNT["stt"], CNT["gp"])
    with nc.Block() as block:
        @block.sync
        def _(s):
            for og in range(NDG):
                if og >= 1:
                    s.wait_ge(SEM["stt"], d_stt0 + og * 8)
                inc("ld3", s.dma_start(
                    a1s[:],
                    a1_scr.rearrange("(j p) c -> p j c", p=128)[
                        :, :, og * 256:(og + 1) * 256]), 16)
                for ss in range(NT):
                    gq = og * 8 + ss
                    if gq >= 2:
                        s.wait_ge(SEM["q"], d_q0 + gq - 1)
                    src = wdt[og, 8 * ss:8 * ss + 8].transpose([1, 0, 2])
                    dst = dfs[:, (gq % 2) * 2048:(gq % 2 + 1) * 2048].rearrange(
                        "p (a c) -> p a c", a=8)
                    inc(("ld", "ld2")[gq % 2], s.dma_start(dst, src), 16)

        @block.vector
        def _(v):
            for og in range(NDG):
                for ss in range(NT):
                    gq = og * 8 + ss
                    v.wait_ge(SEM[("ld", "ld2")[gq % 2]],
                              d_l[("ld", "ld2")[gq % 2]] + 16 * (gq // 2 + 1))
                    if og >= 1 and ss == 0:
                        v.wait_ge(SEM["mm"], d_mm0 + og * 8)
                    bsl = dfs[:, (gq % 2) * 2048:(gq % 2 + 1) * 2048]
                    qsl = wqd[:, ss * 2048:(ss + 1) * 2048]
                    v.tensor_scalar(qsl, bsl, thb[:, 2:3], None, ALU.is_gt)
                    v.tensor_scalar(dtn[:], bsl, nthb[:, 2:3], None, ALU.is_lt)
                    inc("q", v.tensor_tensor(qsl, qsl, dtn[:], ALU.subtract))
                v.wait_ge(SEM["ld3"], d_l["ld3"] + 16 * (og + 1))
                for j in range(NT):
                    m = og * 8 + j
                    v.wait_ge(SEM["mm"], d_mm0 + m + 1)
                    if og >= 1:
                        v.wait_ge(SEM["gp"], d_gp0 + og * 16)
                    ins = v.scalar_tensor_tensor(
                        bst[:, j * 256:(j + 1) * 256],
                        ps_d[:, (j % 2) * 512:(j % 2) * 512 + 256],
                        rqy[:, j:j + 1],
                        a1s[:, j * 256:(j + 1) * 256],
                        ALU.mult, ALU.add)
                    inc("stt", ins)

        @block.tensor
        def _(t):
            for og in range(NDG):
                t.wait_ge(SEM["q"], d_q0 + og * 8 + 8)
                for j in range(NT):
                    m = og * 8 + j
                    if m >= 2:
                        t.wait_ge(SEM["stt"], d_stt0 + m - 1)
                    for ic in range(OC):
                        ins = t.matmul(
                            ps_d[:, (j % 2) * 512:(j % 2) * 512 + 256],
                            yqT[:, j * 8192 + ic * 128: j * 8192 + (ic + 1) * 128],
                            wqd[:, ic * 256:(ic + 1) * 256],
                            start=(ic == 0), stop=(ic == OC - 1))
                    ins.then_inc(SEM["mm"], 1)

        @block.gpsimd
        def _(g):
            for og in range(NDG):
                g.wait_ge(SEM["stt"], d_stt0 + og * 8 + 8)
                inc("gp", g.dma_start(
                    b_scr.rearrange("(j p) c -> p j c", p=128)[
                        :, :, og * 256:(og + 1) * 256],
                    bst[:].rearrange("p (j c) -> p j c", j=NT)), 16)
            g.wait_ge(SEM["gp"], CNT["gp"])
    CNT["q"] = d_q0 + 64
    CNT["mm"] = d_mm0 + 64
    CNT["stt"] = d_stt0 + 64
    es4.close()
    es3a.close()

    es5 = ExitStack()
    lnt = sb(es5, "lnt", [128, 2 * D], F32)
    lns = sb(es5, "lns", [128, D], F32)
    l_ld0, l_ld20 = CNT["ld"], CNT["ld2"]
    l_d0, l_a0, l_st0 = CNT["da"], CNT["aa"], CNT["st"]
    with nc.Block() as block:
        @block.sync
        def _(s):
            for t in range(NT // 2):
                if t >= 1:
                    s.wait_ge(SEM["st"], l_st0 + 32 * t)
                inc("ld", s.dma_start(lnt[:, 0:D],
                                      b_scr[(2 * t) * 128:(2 * t + 1) * 128, :]), 16)
                inc("ld2", s.dma_start(lnt[:, D:2 * D],
                                       b_scr[(2 * t + 1) * 128:(2 * t + 2) * 128, :]), 16)
                s.wait_ge(SEM["da"], l_d0 + 4 * t + 4)
                inc("st", s.dma_start(out[(2 * t) * 128:(2 * t + 1) * 128, :],
                                      lnt[:, 0:D]), 16)
                inc("st", s.dma_start(out[(2 * t + 1) * 128:(2 * t + 2) * 128, :],
                                      lnt[:, D:2 * D]), 16)
            s.wait_ge(SEM["st"], CNT["st"])

        @block.vector
        def _(v):
            for t in range(NT // 2):
                t0 = lnt[:, 0:D]
                t1 = lnt[:, D:2 * D]
                v.wait_ge(SEM["ld"], l_ld0 + 16 * (t + 1))
                v.reduce_sum(tcol[:, 0:1], t0, AX.X)
                v.wait_ge(SEM["ld2"], l_ld20 + 16 * (t + 1))
                v.reduce_sum(tcol[:, 1:2], t1, AX.X)
                v.tensor_scalar(tcol[:, 0:1], tcol[:, 0:1], INV_D, None, ALU.mult)
                v.tensor_scalar(tcol[:, 1:2], tcol[:, 1:2], INV_D, None, ALU.mult)
                sep(v)
                inc("da", v.tensor_scalar(t0, t0, tcol[:, 0:1], None, ALU.subtract))
                inc("da", v.tensor_scalar(t1, t1, tcol[:, 1:2], None, ALU.subtract))
                v.wait_ge(SEM["aa"], l_a0 + 2 * (t + 1))
                v.reciprocal(tcol[:, 2:3], tcol[:, 2:3])
                v.reciprocal(tcol[:, 3:4], tcol[:, 3:4])
                sep(v)
                inc("da", v.tensor_scalar(t0, t0, tcol[:, 2:3], None, ALU.mult))
                inc("da", v.tensor_scalar(t1, t1, tcol[:, 3:4], None, ALU.mult))

        @block.scalar
        def _(sc):
            for t in range(NT // 2):
                t0 = lnt[:, 0:D]
                t1 = lnt[:, D:2 * D]
                sc.wait_ge(SEM["da"], l_d0 + 4 * t + 1)
                sc.activation(lns[:], t0, AFT.Square, bias=0.0, scale=1.0,
                              accum_out=tcol[:, 2:3])
                sc.wait_ge(SEM["da"], l_d0 + 4 * t + 2)
                sc.activation(lns[:], t1, AFT.Square, bias=0.0, scale=1.0,
                              accum_out=tcol[:, 3:4])
                ins = sc.activation(tcol[:, 2:3], tcol[:, 2:3], AFT.Sqrt,
                                    bias=EPS, scale=INV_D)
                ins.then_inc(SEM["aa"], 1)
                ins = sc.activation(tcol[:, 3:4], tcol[:, 3:4], AFT.Sqrt,
                                    bias=EPS, scale=INV_D)
                ins.then_inc(SEM["aa"], 1)
                sc.wait_ge(SEM["da"], l_d0 + 4 * t + 4)
        CNT["aa"] += NT
    es5.close()
    es0.close()
    return nc


def _host_layouts(x, w_gate, w_up, w_down):
    B, N, Dm = x.shape
    xf = np.ascontiguousarray(x.reshape(B * N, Dm))
    idx = np.arange(B * N).reshape(-1, 512)
    deint = np.concatenate([idx[:, 0::2], idx[:, 1::2]], axis=1).reshape(-1)
    xd_full = np.ascontiguousarray(xf[deint])

    wgt = np.ascontiguousarray(
        w_gate.T.reshape(HC, 128, NOG, 512).transpose(2, 0, 1, 3))
    wut = np.ascontiguousarray(
        w_up.T.reshape(HC, 128, NOG, 512).transpose(2, 0, 1, 3))
    wdt = np.ascontiguousarray(
        w_down.T.reshape(OC, 128, NDG, 256).transpose(2, 0, 1, 3))

    h = np.array([[1.0]], dtype=np.float64)
    for _ in range(7):
        h = np.block([[h, h], [h, -h]])
    h128 = h.astype(ml_dtypes.bfloat16)
    idn = np.eye(128, dtype=ml_dtypes.bfloat16)
    idnf = np.eye(128, dtype=np.float32)
    ones_r = np.ones((1, 128), np.float32)
    ones_c = np.ones((128, 1), np.float32)
    return xd_full, deint, wgt, wut, wdt, h128, idn, idnf, ones_r, ones_c


def kernel(L0_attn_x, L0_ffn_w_gate, L0_ffn_w_up, L0_ffn_w_down):
    x = np.asarray(L0_attn_x)
    B, N, Dm = x.shape
    xd_full, deint, wgt, wut, wdt, h128, idn, idnf, ones_r, ones_c = \
        _host_layouts(x, np.asarray(L0_ffn_w_gate), np.asarray(L0_ffn_w_up),
                      np.asarray(L0_ffn_w_down))

    nc = build_program(_STAGE)
    in_maps = []
    for c in range(8):
        xd_c = np.ascontiguousarray(xd_full[c * T:(c + 1) * T])
        in_maps.append(dict(
            xd=xd_c, xtr=np.ascontiguousarray(xd_c.T),
            wgt=wgt, wut=wut, wdt=wdt, h128=h128, idn=idn, idnf=idnf,
            ones_r=ones_r, ones_c=ones_c))
    res = run_bass_kernel_spmd(nc, in_maps, core_ids=list(range(8)))
    of = np.concatenate([r["out"] for r in res.results], axis=0)
    inv = np.empty_like(deint)
    inv[deint] = np.arange(B * N)
    return np.ascontiguousarray(of[inv].reshape(B, N, Dm))


# version 12
# speedup vs baseline: 1.0322x; 1.0044x over previous
from contextlib import ExitStack

import numpy as np
import ml_dtypes
import concourse.bass as bass
import concourse.mybir as mybir
from concourse.bass_utils import run_bass_kernel_spmd

F32 = mybir.dt.float32
BF16 = mybir.dt.bfloat16
FP16 = mybir.dt.float16
ALU = mybir.AluOpType
AFT = mybir.ActivationFunctionType
AX = mybir.AxisListType

T = 1024
D = 2048
HC = 16
OC = 64
NT = 8
NOG = 16
NDG = 8

C_ATT = float(np.float32(128.0) ** np.float32(-0.5))
B_ATT = float(np.float32(np.float32(128.0) ** np.float32(-0.5)) * np.float32(-176.0))
C_FWHT = float(np.float32(1.0 / np.sqrt(8192.0)))
EPS = float(np.float32(1e-5))
INV_D = float(np.float32(1.0 / 2048.0))
INV_NW = float(np.float32(1.0 / 16777216.0))
INV_128 = float(np.float32(1.0 / 128.0))

_STAGE = "full"


def _ladder(v, w0sl, valsl, btsl, outsl):
    for k in range(8, -1, -1):
        v.tensor_scalar(btsl, valsl, float(2 ** k), float(2 ** k), ALU.is_ge, ALU.mult)
        v.tensor_tensor(valsl, valsl, btsl, ALU.subtract)
    v.scalar_tensor_tensor(valsl, w0sl, -128.0, valsl, ALU.add, ALU.subtract)
    v.tensor_scalar(outsl, valsl, -128.0, 127.0, ALU.max, ALU.min)


def build_program(stage="full"):
    nc = bass.Bass("TRN2", target_bir_lowering=False,
                   detect_race_conditions=False)

    xd = nc.dram_tensor("xd", [T, D], F32, kind="ExternalInput").ap()
    xtr = nc.dram_tensor("xtr", [D, T], F32, kind="ExternalInput").ap()
    wgt = nc.dram_tensor("wgt", [NOG, HC, 128, 512], F32, kind="ExternalInput").ap()
    wut = nc.dram_tensor("wut", [NOG, HC, 128, 512], F32, kind="ExternalInput").ap()
    wdt = nc.dram_tensor("wdt", [NDG, OC, 128, 256], F32, kind="ExternalInput").ap()
    h128 = nc.dram_tensor("h128", [128, 128], BF16, kind="ExternalInput").ap()
    idn = nc.dram_tensor("idn", [128, 128], BF16, kind="ExternalInput").ap()
    idnf = nc.dram_tensor("idnf", [128, 128], F32, kind="ExternalInput").ap()
    ones_r = nc.dram_tensor("ones_r", [1, 128], F32, kind="ExternalInput").ap()
    ones_c = nc.dram_tensor("ones_c", [128, 1], F32, kind="ExternalInput").ap()

    out = nc.dram_tensor("out", [T, D], F32, kind="ExternalOutput").ap()

    a1_scr = nc.dram_tensor("a1_scr", [T, D], F32, kind="Internal").ap()
    b_scr = nc.dram_tensor("b_scr", [T, D], F32, kind="Internal").ap()
    i_scr = nc.dram_tensor("i_scr", [128, OC, T], F32, kind="Internal").ap()

    es0 = ExitStack()

    def sb(stk, name, shape, dtype):
        return stk.enter_context(nc.sbuf_tensor(name, shape, dtype))

    def pst(stk, name, shape, dtype):
        return stk.enter_context(nc.psum_tensor(name, shape, dtype))

    cst = sb(es0, "cst", [128, 6], F32)
    h128t = sb(es0, "h128t", [128, 128], BF16)
    idnt = sb(es0, "idnt", [128, 128], BF16)
    idnft = sb(es0, "idnft", [128, 128], F32)
    onesr = sb(es0, "onesr", [1, 128], F32)
    onesc = sb(es0, "onesc", [128, 1], F32)
    asum = sb(es0, "asum", [128, 4], F32)
    thb = sb(es0, "thb", [128, 3], F32)
    nthb = sb(es0, "nthb", [128, 3], F32)
    qv = sb(es0, "qv", [1, 4], F32)
    qgb = sb(es0, "qgb", [128, 1], F32)
    cattb = sb(es0, "cattb", [128, 1], F32)
    qub = sb(es0, "qub", [128, 1], F32)
    qdb = sb(es0, "qdb", [128, 1], F32)
    tiny = sb(es0, "tiny", [1, 8], F32)
    srx = sb(es0, "srx", [128, 8], F32)
    sry = sb(es0, "sry", [128, 8], F32)
    rqy = sb(es0, "rqy", [128, 8], F32)
    gxc = sb(es0, "gxc", [128, 8], F32)
    guc = sb(es0, "guc", [128, 8], F32)
    tcol = sb(es0, "tcol", [128, 4], F32)
    junk = sb(es0, "junk", [128, 256], F32)
    mx8 = sb(es0, "mx8", [128, 8], F32)
    sxv = sb(es0, "sxv", [128, 8], F32)

    nc.const_aps.register(F32, 128.5, cst[:, 0:1])
    nc.const_aps.register(F32, EPS, cst[:, 1:2])
    nc.const_aps.register(F32, INV_D, cst[:, 3:4])
    nc.const_aps.register(F32, B_ATT, cst[:, 4:5])

    es_cat = ExitStack()
    ps_cat = pst(es_cat, "ps_cat", [128, 1], F32)
    SEM = {}
    CNT = {}
    for nm in ("ld", "ld2", "ld3", "ld4", "ld5", "st", "pa", "pb", "pc", "da",
               "db", "dc", "aa", "gp", "gp2", "q", "q2", "mm", "tr", "ev", "si",
               "si2", "stt", "st2", "ld6"):
        SEM[nm] = es0.enter_context(nc.semaphore("s_" + nm))
        CNT[nm] = 0

    def inc(nm, ins, k=1):
        ins.then_inc(SEM[nm], k)
        CNT[nm] += k
        return ins

    def sep(v):
        v.tensor_scalar(junk[:], junk[:], 1.0, None, ALU.mult)

    with nc.Block() as block:
        @block.vector
        def _(v):
            v.memset(cst[:, 0:1], 128.5)
            v.memset(cst[:, 1:2], EPS)
            v.memset(cst[:, 2:3], 128.0)
            v.memset(cst[:, 3:4], INV_D)
            v.memset(cst[:, 4:5], B_ATT)
            v.memset(asum[:, :], 0.0)
            v.memset(junk[:], 1.0)

        @block.sync
        def _(s):
            for src, dst in ((h128, h128t), (idn, idnt), (idnf, idnft),
                             (ones_r, onesr), (ones_c, onesc)):
                inc("ld", s.dma_start(dst[:], src), 16)
            s.wait_ge(SEM["ld"], CNT["ld"])

    with nc.Block() as block:
        @block.vector
        def _(v):
            v.tensor_copy(tiny[:, 7:8], cst[0:1, 2:3])

    with nc.Block() as block:
        @block.scalar
        def _(sc):
            sc.activation(tiny[:, 7:8], tiny[:, 7:8], AFT.Sqrt, bias=0.0, scale=1.0)

    with nc.Block() as block:
        @block.vector
        def _(v):
            v.reciprocal(tiny[:, 7:8], tiny[:, 7:8])

    with nc.Block() as block:
        @block.tensor
        def _(t):
            t.matmul(ps_cat[:], onesr[:], tiny[:, 7:8], start=True, stop=True)

    with nc.Block() as block:
        @block.vector
        def _(v):
            v.tensor_copy(cattb[:], ps_cat[:])
    es_cat.close()

    es_x = ExitStack()
    xqT = sb(es_x, "xqT", [128, HC * T], BF16)

    es1 = ExitStack()
    xdt = sb(es1, "xdt", [128, NT * D], F32)
    xtb = sb(es1, "xtb", [128, HC * T], BF16)
    xdb = sb(es1, "xdb", [128, NT * D], BF16)
    et = sb(es1, "et", [128, 2 * 256], BF16)
    ett = sb(es1, "ett", [128, 2 * 256], BF16)
    rsum = sb(es1, "rsum", [128, 4], F32)
    lsc = sb(es1, "lsc", [128, D], F32)
    mw = sb(es1, "mw", [128, 2 * 2048], F32)
    red = sb(es1, "red", [128, 384], F32)
    l16a = sb(es1, "l16a", [128, 2048], FP16)
    l16b = sb(es1, "l16b", [128, 2048], FP16)
    l16c = sb(es1, "l16c", [128, 2048], FP16)
    ps_s = pst(es1, "ps_s", [128, 2 * 256], F32)
    ps_tr = pst(es1, "ps_tr", [128, 4 * 128], BF16)
    ps_o = pst(es1, "ps_o", [128, 2 * 128], F32)

    mn_si0 = CNT["si"]
    mn_si20 = CNT["si2"]
    MQ = ("ld", "ld2", "ld3", "ld4")
    mn_l = {}
    with nc.Block() as block:
        @block.sync
        def _(s):
            for j in range(NT):
                inc("ld", s.dma_start(xdt[:, j * D:(j + 1) * D],
                                      xd[j * 128:(j + 1) * 128, :]), 16)
            s.wait_ge(SEM["ld"], CNT["ld"])
            for q in MQ:
                mn_l[q] = CNT[q]

        @block.gpsimd
        def _(g):
            for j in range(HC):
                inc("ld5", g.dma_start(xtb[:, j * T:(j + 1) * T],
                                       xtr[j * 128:(j + 1) * 128, :]), 16)
            for j in range(NT):
                inc("ld5", g.dma_start(xdb[:, j * D:(j + 1) * D],
                                       xd[j * 128:(j + 1) * 128, :]), 16)
            g.wait_ge(SEM["ld5"], CNT["ld5"])

    ablocks = [(sg, pr, hh) for sg in range(2) for pr in range(2) for hh in range(8)]
    pa0, pb0, pc0 = CNT["pa"], CNT["pb"], CNT["pc"]
    da0, db0, dc0 = CNT["da"], CNT["db"], CNT["dc"]
    aa0 = CNT["aa"]
    with nc.Block() as block:
        @block.sync
        def _(s):
            k = 0
            for w in (wgt, wut):
                for j in range(16):
                    for qq in range(8):
                        if k >= 4:
                            s.wait_ge(SEM["si"], mn_si0 + k - 3)
                        wsrc = w[j, 2 * qq:2 * qq + 2].transpose([1, 0, 2])
                        dst = mw[:, (k % 4) * 1024:(k % 4 + 1) * 1024].rearrange(
                            "p (a c) -> p a c", a=2)
                        inc(MQ[k % 4], s.dma_start(dst, wsrc), 16)
                        k += 1

        @block.tensor
        def _(t):
            for i, (sg, pr, hh) in enumerate(ablocks):
                head = 2 * hh + pr
                base = 512 * sg + 256 * pr
                gt = xtb[:, head * T + base: head * T + base + 256]
                if i > 0:
                    t.wait_ge(SEM["aa"], aa0 + i)
                for qc in range(2):
                    ins = t.matmul(ps_s[:, qc * 256:(qc + 1) * 256],
                                   gt[:, qc * 128:(qc + 1) * 128], gt,
                                   start=True, stop=True)
                ins.then_inc(SEM["pa"], 1)
                t.wait_ge(SEM["aa"], aa0 + i + 1)
                if i > 0:
                    t.wait_ge(SEM["db"], db0 + i)
                for kc in range(2):
                    for qc in range(2):
                        z = 2 * kc + qc
                        ins = t.transpose(ps_tr[:, z * 128:(z + 1) * 128],
                                          et[:, qc * 256 + kc * 128:
                                             qc * 256 + kc * 128 + 128], idnt[:])
                ins.then_inc(SEM["pb"], 1)
                t.wait_ge(SEM["db"], db0 + i + 1)
                if i > 0:
                    t.wait_ge(SEM["dc"], dc0 + i)
                tb4 = base // 128
                for qc in range(2):
                    for kc in range(2):
                        gk = xdb[:, (tb4 + kc) * D + head * 128:
                                 (tb4 + kc) * D + head * 128 + 128]
                        ins = t.matmul(ps_o[:, qc * 128:(qc + 1) * 128],
                                       ett[:, kc * 256 + qc * 128:
                                           kc * 256 + qc * 128 + 128],
                                       gk, start=(kc == 0), stop=(kc == 1))
                ins.then_inc(SEM["pc"], 1)

        @block.vector
        def _(v):
            for i in range(32):
                sg, pr, hh = ablocks[i]
                head = 2 * hh + pr
                base = 512 * sg + 256 * pr
                v.wait_ge(SEM["pb"], pb0 + i + 1)
                for kc in range(2):
                    for qc in range(2):
                        z = 2 * kc + qc
                        ins = v.tensor_copy(ett[:, kc * 256 + qc * 128:
                                                kc * 256 + qc * 128 + 128],
                                            ps_tr[:, z * 128:(z + 1) * 128])
                ins.then_inc(SEM["db"], 1)
                v.wait_ge(SEM["pc"], pc0 + i + 1)
                tb4 = base // 128
                for qc in range(2):
                    v.reciprocal(rsum[:, qc:qc + 1], rsum[:, qc:qc + 1])
                sep(v)
                for qc in range(2):
                    dst = xdt[:, (tb4 + qc) * D + head * 128:
                              (tb4 + qc) * D + head * 128 + 128]
                    ins = v.scalar_tensor_tensor(dst, ps_o[:, qc * 128:(qc + 1) * 128],
                                                 rsum[:, qc:qc + 1], dst,
                                                 ALU.mult, ALU.add)
                ins.then_inc(SEM["dc"], 1)

        @block.scalar
        def _(sc):
            kch = [0]

            def do_chunks(n):
                for _ in range(n):
                    k = kch[0]
                    sc.wait_ge(SEM[MQ[k % 4]],
                               mn_l[MQ[k % 4]] + 16 * (k // 4 + 1))
                    buf = mw[:, (k % 4) * 1024:(k % 4 + 1) * 1024]
                    ins = sc.activation(buf, buf, AFT.Abs, bias=0.0,
                                        scale=1.0, accum_out=red[:, k:k + 1])
                    ins.then_inc(SEM["si"], 1)
                    kch[0] += 1

            for i in range(32):
                sc.wait_ge(SEM["pa"], pa0 + i + 1)
                if i > 0:
                    sc.wait_ge(SEM["pb"], pb0 + i)
                for qc in range(2):
                    ins = sc.activation(et[:, qc * 256:(qc + 1) * 256],
                                        ps_s[:, qc * 256:(qc + 1) * 256],
                                        AFT.Exp, bias=B_ATT,
                                        scale=cattb[:, 0:1],
                                        accum_out=rsum[:, qc:qc + 1])
                ins.then_inc(SEM["aa"], 1)
                do_chunks(8)
            CNT["si"] = mn_si0 + 256
    CNT["pa"] = pa0 + 32
    CNT["pb"] = pb0 + 32
    CNT["pc"] = pc0 + 32
    CNT["db"] = db0 + 32
    CNT["dc"] = dc0 + 32
    CNT["aa"] = aa0 + 32

    with nc.Block() as block:
        @block.vector
        def _(v):
            for j in range(128):
                for mi in range(2):
                    v.tensor_tensor(asum[:, mi:mi + 1], asum[:, mi:mi + 1],
                                    red[:, mi * 128 + j:mi * 128 + j + 1],
                                    ALU.add)

    es_f = ExitStack()
    ps_fin = pst(es_f, "ps_fin", [128, 8], F32)
    f_pa0, f_da0, f_aa0 = CNT["pa"], CNT["da"], CNT["aa"]
    with nc.Block() as block:
        @block.tensor
        def _(t):
            for mi in range(2):
                ins = t.matmul(ps_fin[0:1, mi:mi + 1], asum[:, mi:mi + 1],
                               onesc[:], start=True, stop=True)
            ins.then_inc(SEM["pa"], 1)
            t.wait_ge(SEM["da"], f_da0 + 2)
            for mi in range(2):
                t.matmul(ps_fin[:, mi:mi + 1], onesr[:], tiny[:, mi:mi + 1],
                         start=True, stop=True)
            for z in range(2):
                ins = t.matmul(ps_fin[:, 3 + z:4 + z], onesr[:], qv[:, z:z + 1],
                               start=True, stop=True)
            ins.then_inc(SEM["pa"], 1)

        @block.vector
        def _(v):
            v.wait_ge(SEM["pa"], f_pa0 + 1)
            for mi in range(2):
                v.tensor_copy(tiny[:, mi:mi + 1], ps_fin[0:1, mi:mi + 1])
            sep(v)
            for mi in range(2):
                v.tensor_scalar(tiny[:, mi:mi + 1], tiny[:, mi:mi + 1],
                                INV_NW, EPS, ALU.mult, ALU.max)
            sep(v)
            for mi in range(2):
                v.reciprocal(tiny[:, 4 + mi:5 + mi], tiny[:, mi:mi + 1])
            sep(v)
            for mi in range(2):
                v.reciprocal(qv[:, mi:mi + 1], tiny[:, 4 + mi:5 + mi])
            sep(v)
            for mi in range(2):
                ins = v.tensor_scalar(tiny[:, mi:mi + 1], tiny[:, mi:mi + 1],
                                      0.5, None, ALU.mult)
            ins.then_inc(SEM["da"], 1)
            ins = v.tensor_scalar(junk[:, 0:1], junk[:, 0:1], 1.0, None, ALU.add)
            ins.then_inc(SEM["da"], 1)
            v.wait_ge(SEM["pa"], f_pa0 + 2)
            for mi in range(2):
                v.tensor_copy(thb[:, mi:mi + 1], ps_fin[:, mi:mi + 1])
                v.tensor_scalar(nthb[:, mi:mi + 1], ps_fin[:, mi:mi + 1],
                                -1.0, None, ALU.mult)
            v.tensor_copy(qgb[:], ps_fin[:, 3:4])
            v.tensor_copy(qub[:], ps_fin[:, 4:5])

    CNT["pa"] = f_pa0 + 2
    CNT["da"] = f_da0 + 2
    es_f.close()


    if stage == "attn":
        with nc.Block() as block:
            @block.sync
            def _(s):
                for j in range(NT):
                    inc("st", s.dma_start(out[j * 128:(j + 1) * 128, :],
                                          xdt[:, j * D:(j + 1) * D]), 16)
                s.wait_ge(SEM["st"], CNT["st"])
        es1.close(); es_x.close(); es0.close()
        return nc

    ln_d0, ln_a0 = CNT["da"], CNT["aa"]
    with nc.Block() as block:
        @block.vector
        def _(v):
            for t in range(NT // 2):
                t0 = xdt[:, (2 * t) * D:(2 * t + 1) * D]
                t1 = xdt[:, (2 * t + 1) * D:(2 * t + 2) * D]
                v.reduce_sum(tcol[:, 0:1], t0, AX.X)
                v.reduce_sum(tcol[:, 1:2], t1, AX.X)
                v.tensor_scalar(tcol[:, 0:1], tcol[:, 0:1], INV_D, None, ALU.mult)
                v.tensor_scalar(tcol[:, 1:2], tcol[:, 1:2], INV_D, None, ALU.mult)
                sep(v)
                inc("da", v.tensor_scalar(t0, t0, tcol[:, 0:1], None, ALU.subtract))
                inc("da", v.tensor_scalar(t1, t1, tcol[:, 1:2], None, ALU.subtract))
                v.wait_ge(SEM["aa"], ln_a0 + 2 * (t + 1))
                v.reciprocal(tcol[:, 2:3], tcol[:, 2:3])
                v.reciprocal(tcol[:, 3:4], tcol[:, 3:4])
                sep(v)
                inc("da", v.tensor_scalar(t0, t0, tcol[:, 2:3], None, ALU.mult))
                inc("da", v.tensor_scalar(t1, t1, tcol[:, 3:4], None, ALU.mult))

        @block.scalar
        def _(sc):
            for t in range(NT // 2):
                t0 = xdt[:, (2 * t) * D:(2 * t + 1) * D]
                t1 = xdt[:, (2 * t + 1) * D:(2 * t + 2) * D]
                sc.wait_ge(SEM["da"], ln_d0 + 4 * t + 1)
                sc.activation(lsc[:], t0, AFT.Square, bias=0.0, scale=1.0,
                              accum_out=tcol[:, 2:3])
                sc.wait_ge(SEM["da"], ln_d0 + 4 * t + 2)
                sc.activation(lsc[:], t1, AFT.Square, bias=0.0, scale=1.0,
                              accum_out=tcol[:, 3:4])
                ins = sc.activation(tcol[:, 2:3], tcol[:, 2:3], AFT.Sqrt,
                                    bias=EPS, scale=INV_D)
                ins.then_inc(SEM["aa"], 1)
                ins = sc.activation(tcol[:, 3:4], tcol[:, 3:4], AFT.Sqrt,
                                    bias=EPS, scale=INV_D)
                ins.then_inc(SEM["aa"], 1)
                sc.wait_ge(SEM["da"], ln_d0 + 4 * t + 4)
        CNT["aa"] += NT

    with nc.Block() as block:
        @block.gpsimd
        def _(g):
            for j in range(NT):
                inc("st2", g.dma_start(a1_scr[j * 128:(j + 1) * 128, :],
                                       xdt[:, j * D:(j + 1) * D]), 16)
            g.wait_ge(SEM["st2"], CNT["st2"])

    if stage == "a1":
        with nc.Block() as block:
            @block.sync
            def _(s):
                for j in range(NT):
                    inc("st", s.dma_start(out[j * 128:(j + 1) * 128, :],
                                          xdt[:, j * D:(j + 1) * D]), 16)
                s.wait_ge(SEM["st"], CNT["st"])
        es1.close(); es_x.close(); es0.close()
        return nc

    xq_d0, xq_a0 = CNT["da"], CNT["aa"]
    with nc.Block() as block:
        @block.vector
        def _(v):
            for j in range(NT):
                v.reduce_max(mx8[:, j:j + 1], xdt[:, j * D:(j + 1) * D], AX.X,
                             apply_absolute_value=True)
            for j in range(NT):
                v.tensor_scalar(srx[:, j:j + 1], mx8[:, j:j + 1], EPS, INV_128,
                                ALU.max, ALU.mult)
            for j in range(NT):
                v.reciprocal(sxv[:, j:j + 1], srx[:, j:j + 1])
            for j in range(NT):
                v.tensor_tensor(gxc[:, j:j + 1], srx[:, j:j + 1], qgb[:], ALU.mult)
            for j in range(NT):
                ins = v.tensor_tensor(guc[:, j:j + 1], srx[:, j:j + 1], qub[:],
                                      ALU.mult)
            inc("da", ins)
            for j in range(NT):
                tile = xdt[:, j * D:(j + 1) * D]
                v.tensor_scalar(l16a[:], tile, sxv[:, j:j + 1], 128.5,
                                ALU.mult, ALU.add)
                v.tensor_scalar(l16b[:], tile, sxv[:, j:j + 1], 128.5,
                                ALU.mult, ALU.add)
                _ladder(v, l16a[:], l16b[:], l16c[:], xdb[:, j * D:(j + 1) * D])

    tr_p0, tr_d0 = CNT["pa"], CNT["da"]
    es_tp = ExitStack()
    ps_x = pst(es_tp, "ps_x", [128, 2 * 1024], BF16)
    with nc.Block() as block:
        @block.tensor
        def _(t):
            g = 0
            for j in range(NT):
                for hcg in range(4):
                    if g >= 2:
                        t.wait_ge(SEM["da"], tr_d0 + g - 1)
                    for z in range(4):
                        hc = hcg * 4 + z
                        ins = t.transpose(
                            ps_x[:, (g % 2) * 1024 + z * 128:
                                 (g % 2) * 1024 + (z + 1) * 128],
                            xdb[:, j * D + hc * 128: j * D + hc * 128 + 128], idnt[:])
                    ins.then_inc(SEM["pa"], 1)
                    g += 1

        @block.vector
        def _(v):
            g = 0
            for j in range(NT):
                for hcg in range(4):
                    v.wait_ge(SEM["pa"], tr_p0 + g + 1)
                    for z in range(4):
                        hc = hcg * 4 + z
                        ins = v.tensor_copy(
                            xqT[:, hc * T + j * 128: hc * T + (j + 1) * 128],
                            ps_x[:, (g % 2) * 1024 + z * 128:
                                 (g % 2) * 1024 + (z + 1) * 128])
                    ins.then_inc(SEM["da"], 1)
                    g += 1
    CNT["pa"] = tr_p0 + 32
    CNT["da"] = tr_d0 + 32
    es_tp.close()

    if stage == "xq":
        with nc.Block() as block:
            @block.vector
            def _(v):
                v.tensor_copy(lsc[:, 0:1024], xqT[:, 0:T])
        with nc.Block() as block:
            @block.sync
            def _(s):
                inc("st", s.dma_start(out[0:128, 0:1024], lsc[:, 0:1024]), 16)
                s.wait_ge(SEM["st"], CNT["st"])
        es1.close(); es_x.close(); es0.close()
        return nc

    es1.close()

    es2 = ExitStack()
    wfg = sb(es2, "wfg", [128, 2 * 2048], F32)
    wfu = sb(es2, "wfu", [128, 2 * 2048], F32)
    wqg = sb(es2, "wqg", [128, 2 * 8192], BF16)
    wqu = sb(es2, "wqu", [128, 2 * 8192], BF16)
    tpb = sb(es2, "tpb", [128, 2048], BF16)
    tnb = sb(es2, "tnb", [128, 2048], BF16)
    sgt = sb(es2, "sgt", [128, 2 * T], F32)
    ist = sb(es2, "ist", [128, 2 * 4 * T], F32)
    grwx = sb(es2, "grwx", [1, T], F32)
    gxB = sb(es2, "gxB", [128, T], F32)
    mwd = sb(es2, "mwd", [128, 4 * 1024], F32)
    red2 = sb(es2, "red2", [128, 128], F32)
    ps_g = pst(es2, "ps_g", [128, 2 * T], F32)
    ps_u = pst(es2, "ps_u", [128, 2 * T], F32)

    gx_p0, gx_d0 = CNT["pa"], CNT["da"]
    with nc.Block() as block:
        @block.tensor
        def _(t):
            for j in range(NT):
                ins = t.transpose(ps_g[0:1, j * 128:(j + 1) * 128],
                                  srx[:, j:j + 1], idnft[:])
            ins.then_inc(SEM["pa"], 1)
            t.wait_ge(SEM["da"], gx_d0 + 1)
            for hf in range(2):
                ins = t.matmul(ps_u[:, hf * 512:(hf + 1) * 512], onesr[:],
                               grwx[0:1, hf * 512:(hf + 1) * 512],
                               start=True, stop=True)
            ins.then_inc(SEM["pa"], 1)

        @block.vector
        def _(v):
            v.wait_ge(SEM["pa"], gx_p0 + 1)
            for j in range(NT):
                v.tensor_copy(grwx[0:1, j * 128:(j + 1) * 128],
                              ps_g[0:1, j * 128:(j + 1) * 128])
            sep(v)
            ins = v.tensor_scalar(grwx[0:1, :], grwx[0:1, :], qv[:, 0:1],
                                  None, ALU.mult)
            ins.then_inc(SEM["da"], 1)
            v.wait_ge(SEM["pa"], gx_p0 + 2)
            v.tensor_copy(gxB[:, 0:512], ps_u[:, 0:512])
            v.tensor_copy(gxB[:, 512:1024], ps_u[:, 512:1024])
    CNT["pa"] = gx_p0 + 2
    CNT["da"] = gx_d0 + 1

    g_l = {nm: CNT[nm] for nm in ("ld", "ld2", "ld3", "ld4")}
    d_l60, d_l50, d_si20 = CNT["ld6"], CNT["ld5"], CNT["si2"]
    g_q0, g_q20 = CNT["q"], CNT["q2"]
    g_mm0, g_si0, g_ev0 = CNT["mm"], CNT["si"], CNT["ev"]
    g_gp0, g_gp20 = CNT["gp"], CNT["gp2"]
    g_gr0 = CNT["dc"]
    with nc.Block() as block:
        @block.sync
        def _(s):
            for og in range(NOG):
                for qq in range(4):
                    gq = og * 4 + qq
                    if gq >= 2:
                        s.wait_ge(SEM["q"], g_q0 + gq - 1)
                    src = wgt[og, 4 * qq:4 * qq + 4].transpose([1, 0, 2])
                    dst = wfg[:, (gq % 2) * 2048:(gq % 2 + 1) * 2048].rearrange(
                        "p (a c) -> p a c", a=4)
                    inc(("ld", "ld2")[qq % 2], s.dma_start(dst, src), 16)
                for qq in range(4):
                    gq = og * 4 + qq
                    if gq >= 2:
                        s.wait_ge(SEM["q2"], g_q20 + gq - 1)
                    src = wut[og, 4 * qq:4 * qq + 4].transpose([1, 0, 2])
                    dst = wfu[:, (gq % 2) * 2048:(gq % 2 + 1) * 2048].rearrange(
                        "p (a c) -> p a c", a=4)
                    inc(("ld3", "ld4")[qq % 2], s.dma_start(dst, src), 16)

        @block.vector
        def _(v):
            for og in range(NOG + 1):
                if og < NOG:
                    if og >= 2:
                        v.wait_ge(SEM["mm"], g_mm0 + (og - 1) * 4)
                    for half in range(2):
                        qn = "q" if half == 0 else "q2"
                        buf = wfg if half == 0 else wfu
                        wq = wqg if half == 0 else wqu
                        for qq in range(4):
                            gq = og * 4 + qq
                            lnm = (("ld", "ld2") if half == 0 else ("ld3", "ld4"))[qq % 2]
                            v.wait_ge(SEM[lnm], g_l[lnm] + 16 * (og * 2 + qq // 2 + 1))
                            bsl = buf[:, (gq % 2) * 2048:(gq % 2 + 1) * 2048]
                            qsl = wq[:, (og % 2) * 8192 + qq * 2048:
                                     (og % 2) * 8192 + (qq + 1) * 2048]
                            v.tensor_scalar(tpb[:], bsl, thb[:, half:half + 1],
                                            None, ALU.is_gt)
                            v.tensor_scalar(tnb[:], bsl, nthb[:, half:half + 1],
                                            None, ALU.is_lt)
                            inc(qn, v.tensor_tensor(qsl, tpb[:], tnb[:], ALU.subtract))
                if og >= 1:
                    po = og - 1
                    for oc4 in range(4):
                        m = po * 4 + oc4
                        p = m % 2
                        v.wait_ge(SEM["mm"], g_mm0 + m + 1)
                        ins = v.tensor_tensor(sgt[:, p * T:p * T + T],
                                              ps_g[:, p * T:p * T + T],
                                              gxB[:], ALU.mult)
                        inc("dc", ins)
                        v.wait_ge(SEM["si"], g_si0 + m + 1)
                        if po >= 2:
                            v.wait_ge(SEM[("gp", "gp2")[po % 2]],
                                      (g_gp0, g_gp20)[po % 2] + 64 * ((po - 2) // 2 + 1))
                        ins = v.tensor_tensor(
                            ist[:, (po % 2) * 4 * T + oc4 * T:
                                (po % 2) * 4 * T + (oc4 + 1) * T],
                            sgt[:, p * T:p * T + T],
                            ps_u[:, p * T:p * T + T], ALU.mult)
                        inc("ev", ins)

        @block.tensor
        def _(t):
            for og in range(NOG):
                t.wait_ge(SEM["q"], g_q0 + og * 4 + 4)
                t.wait_ge(SEM["q2"], g_q20 + og * 4 + 4)
                for oc4 in range(4):
                    m = og * 4 + oc4
                    p = m % 2
                    if m >= 2:
                        t.wait_ge(SEM["ev"], g_ev0 + m - 1)
                    for half in range(2):
                        wq = wqg if half == 0 else wqu
                        pdst = ps_g if half == 0 else ps_u
                        for hc in range(HC):
                            lh = wq[:, (og % 2) * 8192 + hc * 512 + oc4 * 128:
                                    (og % 2) * 8192 + hc * 512 + oc4 * 128 + 128]
                            ins = t.matmul(pdst[:, p * T:p * T + 512], lh,
                                           xqT[:, hc * T:hc * T + 512],
                                           start=(hc == 0), stop=(hc == HC - 1))
                            ins = t.matmul(pdst[:, p * T + 512:p * T + 1024], lh,
                                           xqT[:, hc * T + 512:hc * T + 1024],
                                           start=(hc == 0), stop=(hc == HC - 1))
                    ins.then_inc(SEM["mm"], 1)

        @block.scalar
        def _(sc):
            for og in range(NOG):
                for oc4 in range(4):
                    m = og * 4 + oc4
                    p = m % 2
                    sc.wait_ge(SEM["dc"], g_gr0 + m + 1)
                    ins = sc.activation(sgt[:, p * T:p * T + T],
                                        sgt[:, p * T:p * T + T],
                                        AFT.Silu, bias=0.0, scale=1.0)
                    ins.then_inc(SEM["si"], 1)
                    for kc in (2 * m, 2 * m + 1):
                        qn6 = ("ld6", "ld5")[kc % 2]
                        b6 = (d_l60, d_l50)[kc % 2]
                        sc.wait_ge(SEM[qn6], b6 + 16 * (kc // 2 + 1))
                        buf = mwd[:, (kc % 4) * 1024:(kc % 4 + 1) * 1024]
                        ins = sc.activation(buf, buf, AFT.Abs, bias=0.0,
                                            scale=1.0,
                                            accum_out=red2[:, kc:kc + 1])
                        ins.then_inc(SEM["si2"], 1)
            CNT["si2"] = d_si20 + 128

        @block.gpsimd
        def _(g):
            for og in range(NOG):
                for c in range(8):
                    kc = og * 8 + c
                    if kc >= 4:
                        g.wait_ge(SEM["si2"], d_si20 + kc - 3)
                    j, qq = kc // 8, kc % 8
                    src = wdt[j // 2, (j % 2) * 32 + 4 * qq:
                              (j % 2) * 32 + 4 * qq + 4].transpose([1, 0, 2])
                    dst = mwd[:, (kc % 4) * 1024:
                              (kc % 4 + 1) * 1024].rearrange(
                        "p (a c) -> p a c", a=4)
                    inc(("ld6", "ld5")[kc % 2], g.dma_start(dst, src), 16)
                g.wait_ge(SEM["ev"], g_ev0 + og * 4 + 4)
                for z in range(4):
                    inc(("gp", "gp2")[og % 2], g.dma_start(
                        i_scr[:, og * 4 + z:og * 4 + z + 1, :].squeeze(1),
                        ist[:, (og % 2) * 4 * T + z * T:
                            (og % 2) * 4 * T + (z + 1) * T]), 16)
            g.wait_ge(SEM["gp"], CNT["gp"])
            g.wait_ge(SEM["gp2"], CNT["gp2"])
    CNT["q"] = g_q0 + 64
    CNT["q2"] = g_q20 + 64
    CNT["mm"] = g_mm0 + 64
    CNT["si"] = g_si0 + 64
    CNT["dc"] = g_gr0 + 64
    CNT["ev"] = g_ev0 + 64

    with nc.Block() as block:
        @block.vector
        def _(v):
            for j in range(64):
                v.tensor_tensor(asum[:, 2:3], asum[:, 2:3],
                                red2[:, 2 * j:2 * j + 1], ALU.add)
                v.tensor_tensor(asum[:, 3:4], asum[:, 3:4],
                                red2[:, 2 * j + 1:2 * j + 2], ALU.add)
            sep(v)
            v.tensor_tensor(asum[:, 2:3], asum[:, 2:3], asum[:, 3:4], ALU.add)

    fb_pa0, fb_da0 = CNT["pa"], CNT["da"]
    with nc.Block() as block:
        @block.tensor
        def _(t):
            ins = t.matmul(ps_g[0:1, 0:1], asum[:, 2:3], onesc[:],
                           start=True, stop=True)
            ins.then_inc(SEM["pa"], 1)
            t.wait_ge(SEM["da"], fb_da0 + 2)
            t.matmul(ps_g[:, 0:1], onesr[:], tiny[:, 2:3],
                     start=True, stop=True)
            ins = t.matmul(ps_g[:, 1:2], onesr[:], qv[:, 2:3],
                           start=True, stop=True)
            ins.then_inc(SEM["pa"], 1)

        @block.vector
        def _(v):
            v.wait_ge(SEM["pa"], fb_pa0 + 1)
            v.tensor_copy(tiny[:, 2:3], ps_g[0:1, 0:1])
            sep(v)
            v.tensor_scalar(tiny[:, 2:3], tiny[:, 2:3],
                            INV_NW, EPS, ALU.mult, ALU.max)
            sep(v)
            v.reciprocal(tiny[:, 6:7], tiny[:, 2:3])
            sep(v)
            v.reciprocal(qv[:, 2:3], tiny[:, 6:7])
            sep(v)
            ins = v.tensor_scalar(tiny[:, 2:3], tiny[:, 2:3],
                                  0.5, None, ALU.mult)
            ins.then_inc(SEM["da"], 1)
            ins = v.tensor_scalar(junk[:, 0:1], junk[:, 0:1], 1.0, None, ALU.add)
            ins.then_inc(SEM["da"], 1)
            v.wait_ge(SEM["pa"], fb_pa0 + 2)
            v.tensor_copy(thb[:, 2:3], ps_g[:, 0:1])
            v.tensor_scalar(nthb[:, 2:3], ps_g[:, 0:1], -1.0, None, ALU.mult)
            v.tensor_copy(qdb[:], ps_g[:, 1:2])
    CNT["pa"] = fb_pa0 + 2
    CNT["da"] = fb_da0 + 2

    es2.close()
    es_x.close()

    es3a = ExitStack()
    yqT = sb(es3a, "yqT", [128, OC * T], BF16)
    es3 = ExitStack()
    ibuf = sb(es3, "ibuf", [128, OC * 128], F32)
    l16a = sb(es3, "m16a", [128, 2048], FP16)
    l16b = sb(es3, "m16b", [128, 2048], FP16)
    l16c = sb(es3, "m16c", [128, 2048], FP16)
    hib = sb(es3, "hib", [128, 2 * 512], BF16)
    lob = sb(es3, "lob", [128, 2 * 512], BF16)
    mby = sb(es3, "mby", [128, 128], F32)
    csr = sb(es3, "csr", [1, 128], F32)
    scb = sb(es3, "scb", [128, 128], F32)
    ps_z = pst(es3, "ps_z", [128, 2 * 512], F32)
    ps_m = pst(es3, "ps_m", [128, 128], F32)
    ps_r = pst(es3, "ps_r", [1, 128], F32)
    ps_b = pst(es3, "ps_b", [128, 128], F32)

    fw_ld0 = CNT["ld"]
    fw_ev0 = CNT["ev"]
    for tb in range(NT):
        if stage == "fw1" and tb == 1:
            with nc.Block() as block:
                @block.sync
                def _(s):
                    for ch in range(4):
                        inc("st", s.dma_start(out[ch * 128:(ch + 1) * 128, 0:2048],
                                              ibuf[:, ch * 2048:(ch + 1) * 2048]), 16)
                    s.wait_ge(SEM["st"], CNT["st"])
            es3.close(); es3a.close(); es0.close()
            return nc

        f_q0, f_p0, f_e0 = CNT["q"], CNT["pa"], CNT["db"]
        f_pb0, f_dc0 = CNT["pb"], CNT["dc"]
        with nc.Block() as block:
            @block.sync
            def _(s):
                if tb >= 1:
                    s.wait_ge(SEM["ev"], fw_ev0 + tb)
                inc("ld", s.dma_start(
                    ibuf[:].rearrange("p (a t) -> p a t", a=OC),
                    i_scr[:, :, tb * 128:(tb + 1) * 128]), 16)

            @block.vector
            def _(v):
                v.wait_ge(SEM["ld"], CNT["ld"])
                for st_ in range(6):
                    ngrp = OC // (2 ** (st_ + 1))
                    vw = ibuf[:].rearrange("p (g two l) -> p g two l",
                                           g=ngrp, two=2)
                    lo = vw[:, :, 0, :]
                    hi = vw[:, :, 1, :]
                    v.tensor_tensor(lo, lo, hi, ALU.add)
                    v.scalar_tensor_tensor(hi, hi, -2.0, lo, ALU.mult, ALU.add)
                for gg in range(16):
                    if gg >= 2:
                        v.wait_ge(SEM["pa"], f_p0 + gg - 1)
                    sl = ibuf[:, gg * 512:(gg + 1) * 512]
                    hsl = hib[:, (gg % 2) * 512:(gg % 2 + 1) * 512]
                    lsl = lob[:, (gg % 2) * 512:(gg % 2 + 1) * 512]
                    v.tensor_copy(hsl, sl)
                    inc("q", v.tensor_tensor(lsl, sl, hsl, ALU.subtract))
                    if gg >= 1:
                        v.wait_ge(SEM["pa"], f_p0 + gg)
                        pg = gg - 1
                        inc("db", v.tensor_copy(
                            ibuf[:, pg * 512:(pg + 1) * 512],
                            ps_z[:, (pg % 2) * 512:(pg % 2 + 1) * 512]))
                v.wait_ge(SEM["pa"], f_p0 + 16)
                inc("db", v.tensor_copy(ibuf[:, 15 * 512:16 * 512],
                                        ps_z[:, 512:1024]))
                ins = v.tensor_reduce(mby[:], ibuf[:].rearrange(
                    "p (a t) -> p t a", a=OC), AX.X, ALU.max,
                    apply_absolute_value=True)
                inc("dc", ins)
                v.wait_ge(SEM["pb"], f_pb0 + 1)
                v.reduce_max(tcol[:, 0:1], ps_m[:], AX.X)
                sep(v)
                v.tensor_tensor(tcol[:, 0:1], tcol[:, 0:1], guc[:, tb:tb + 1],
                                ALU.mult)
                sep(v)
                v.tensor_scalar(tcol[:, 0:1], tcol[:, 0:1], C_FWHT, EPS,
                                ALU.mult, ALU.max)
                sep(v)
                v.tensor_scalar(sry[:, tb:tb + 1], tcol[:, 0:1], INV_128, None,
                                ALU.mult)
                sep(v)
                v.tensor_tensor(rqy[:, tb:tb + 1], sry[:, tb:tb + 1], qdb[:],
                                ALU.mult)
                sep(v)
                v.reciprocal(tcol[:, 1:2], sry[:, tb:tb + 1])
                sep(v)
                v.tensor_scalar(tcol[:, 1:2], tcol[:, 1:2], C_FWHT, None,
                                ALU.mult)
                sep(v)
                ins = v.tensor_tensor(tcol[:, 1:2], tcol[:, 1:2], guc[:, tb:tb + 1],
                                      ALU.mult)
                inc("dc", ins)
                v.wait_ge(SEM["pb"], f_pb0 + 2)
                inc("dc", v.tensor_copy(csr[:], ps_r[:]))
                v.wait_ge(SEM["pb"], f_pb0 + 3)
                v.tensor_copy(scb[:], ps_b[:])
                for z in range(4):
                    sl = ibuf[:, z * 2048:(z + 1) * 2048]
                    slv = sl.rearrange("p (a t) -> p a t", a=16)
                    bcast = scb[:].unsqueeze(1).broadcast_to([128, 16, 128])
                    v.tensor_tensor(slv, slv, bcast, ALU.mult)
                    v.tensor_scalar(l16a[:], sl, 128.5, None, ALU.add)
                    v.tensor_scalar(l16b[:], sl, 128.5, None, ALU.add)
                    yout = yqT[:, tb * 8192 + z * 2048: tb * 8192 + (z + 1) * 2048]
                    _ladder(v, l16a[:], l16b[:], l16c[:], yout)
                inc("ev", v.nop())

            @block.tensor
            def _(t):
                for gg in range(16):
                    t.wait_ge(SEM["q"], f_q0 + gg + 1)
                    if gg >= 2:
                        t.wait_ge(SEM["db"], f_e0 + gg - 1)
                    hsl = hib[:, (gg % 2) * 512:(gg % 2 + 1) * 512]
                    lsl = lob[:, (gg % 2) * 512:(gg % 2 + 1) * 512]
                    t.matmul(ps_z[:, (gg % 2) * 512:(gg % 2 + 1) * 512],
                             h128t[:], hsl, start=True, stop=False)
                    ins = t.matmul(ps_z[:, (gg % 2) * 512:(gg % 2 + 1) * 512],
                                   h128t[:], lsl, start=False, stop=True)
                    ins.then_inc(SEM["pa"], 1)
                t.wait_ge(SEM["dc"], f_dc0 + 1)
                ins = t.transpose(ps_m[:], mby[:], idnft[:])
                ins.then_inc(SEM["pb"], 1)
                t.wait_ge(SEM["dc"], f_dc0 + 2)
                ins = t.transpose(ps_r[:], tcol[:, 1:2], idnft[:])
                ins.then_inc(SEM["pb"], 1)
                t.wait_ge(SEM["dc"], f_dc0 + 3)
                ins = t.matmul(ps_b[:], onesr[:], csr[:], start=True, stop=True)
                ins.then_inc(SEM["pb"], 1)
        CNT["q"] = f_q0 + 16
        CNT["pa"] = f_p0 + 16
        CNT["db"] = f_e0 + 16
        CNT["pb"] = f_pb0 + 3
        CNT["dc"] = f_dc0 + 3
        CNT["ev"] = fw_ev0 + tb + 1

    if stage == "yq":
        with nc.Block() as block:
            @block.vector
            def _(v):
                v.tensor_copy(ibuf[:, 4096:6144], yqT[:, 0:2048])
                v.tensor_copy(ibuf[:, 0:2048], yqT[:, 2048:4096])
        with nc.Block() as block:
            @block.sync
            def _(s):
                inc("st", s.dma_start(out[0:128, 0:2048], ibuf[:, 4096:6144]), 16)
                inc("st", s.dma_start(out[128:256, 0:2048], ibuf[:, 0:2048]), 16)
                inc("st", s.dma_start(out[256:384, 0:8], sry[:, :]), 16)
                s.wait_ge(SEM["st"], CNT["st"])
        es3.close(); es3a.close(); es0.close()
        return nc

    es3.close()

    es4 = ExitStack()
    dfs = sb(es4, "dfs", [128, 2 * 2048], F32)
    wqd = sb(es4, "wqd", [128, OC * 256], BF16)
    dtn = sb(es4, "dtn", [128, 2048], BF16)
    a1s = sb(es4, "a1s", [128, NT * 256], F32)
    bst = sb(es4, "bst", [128, NT * 256], F32)
    ps_d = pst(es4, "ps_d", [128, 2 * 512], F32)

    d_l = {nm: CNT[nm] for nm in ("ld", "ld2", "ld3")}
    d_q0, d_mm0, d_stt0, d_gp0 = (CNT["q"], CNT["mm"], CNT["stt"], CNT["gp"])
    with nc.Block() as block:
        @block.sync
        def _(s):
            for og in range(NDG):
                if og >= 1:
                    s.wait_ge(SEM["stt"], d_stt0 + og * 8)
                inc("ld3", s.dma_start(
                    a1s[:],
                    a1_scr.rearrange("(j p) c -> p j c", p=128)[
                        :, :, og * 256:(og + 1) * 256]), 16)
                for ss in range(NT):
                    gq = og * 8 + ss
                    if gq >= 2:
                        s.wait_ge(SEM["q"], d_q0 + gq - 1)
                    src = wdt[og, 8 * ss:8 * ss + 8].transpose([1, 0, 2])
                    dst = dfs[:, (gq % 2) * 2048:(gq % 2 + 1) * 2048].rearrange(
                        "p (a c) -> p a c", a=8)
                    inc(("ld", "ld2")[gq % 2], s.dma_start(dst, src), 16)

        @block.vector
        def _(v):
            for og in range(NDG):
                for ss in range(NT):
                    gq = og * 8 + ss
                    v.wait_ge(SEM[("ld", "ld2")[gq % 2]],
                              d_l[("ld", "ld2")[gq % 2]] + 16 * (gq // 2 + 1))
                    if og >= 1 and ss == 0:
                        v.wait_ge(SEM["mm"], d_mm0 + og * 8)
                    bsl = dfs[:, (gq % 2) * 2048:(gq % 2 + 1) * 2048]
                    qsl = wqd[:, ss * 2048:(ss + 1) * 2048]
                    v.tensor_scalar(qsl, bsl, thb[:, 2:3], None, ALU.is_gt)
                    v.tensor_scalar(dtn[:], bsl, nthb[:, 2:3], None, ALU.is_lt)
                    inc("q", v.tensor_tensor(qsl, qsl, dtn[:], ALU.subtract))
                v.wait_ge(SEM["ld3"], d_l["ld3"] + 16 * (og + 1))
                for j in range(NT):
                    m = og * 8 + j
                    v.wait_ge(SEM["mm"], d_mm0 + m + 1)
                    if og >= 1:
                        v.wait_ge(SEM["gp"], d_gp0 + og * 16)
                    ins = v.scalar_tensor_tensor(
                        bst[:, j * 256:(j + 1) * 256],
                        ps_d[:, (j % 2) * 512:(j % 2) * 512 + 256],
                        rqy[:, j:j + 1],
                        a1s[:, j * 256:(j + 1) * 256],
                        ALU.mult, ALU.add)
                    inc("stt", ins)

        @block.tensor
        def _(t):
            for og in range(NDG):
                t.wait_ge(SEM["q"], d_q0 + og * 8 + 8)
                for j in range(NT):
                    m = og * 8 + j
                    if m >= 2:
                        t.wait_ge(SEM["stt"], d_stt0 + m - 1)
                    for ic in range(OC):
                        ins = t.matmul(
                            ps_d[:, (j % 2) * 512:(j % 2) * 512 + 256],
                            yqT[:, j * 8192 + ic * 128: j * 8192 + (ic + 1) * 128],
                            wqd[:, ic * 256:(ic + 1) * 256],
                            start=(ic == 0), stop=(ic == OC - 1))
                    ins.then_inc(SEM["mm"], 1)

        @block.gpsimd
        def _(g):
            for og in range(NDG):
                g.wait_ge(SEM["stt"], d_stt0 + og * 8 + 8)
                inc("gp", g.dma_start(
                    b_scr.rearrange("(j p) c -> p j c", p=128)[
                        :, :, og * 256:(og + 1) * 256],
                    bst[:].rearrange("p (j c) -> p j c", j=NT)), 16)
            g.wait_ge(SEM["gp"], CNT["gp"])
    CNT["q"] = d_q0 + 64
    CNT["mm"] = d_mm0 + 64
    CNT["stt"] = d_stt0 + 64
    es4.close()
    es3a.close()

    es5 = ExitStack()
    lnt = sb(es5, "lnt", [128, 2 * D], F32)
    lns = sb(es5, "lns", [128, D], F32)
    l_ld0, l_ld20 = CNT["ld"], CNT["ld2"]
    l_d0, l_a0, l_st0 = CNT["da"], CNT["aa"], CNT["st"]
    with nc.Block() as block:
        @block.sync
        def _(s):
            for t in range(NT // 2):
                if t >= 1:
                    s.wait_ge(SEM["st"], l_st0 + 32 * t)
                inc("ld", s.dma_start(lnt[:, 0:D],
                                      b_scr[(2 * t) * 128:(2 * t + 1) * 128, :]), 16)
                inc("ld2", s.dma_start(lnt[:, D:2 * D],
                                       b_scr[(2 * t + 1) * 128:(2 * t + 2) * 128, :]), 16)
                s.wait_ge(SEM["da"], l_d0 + 4 * t + 4)
                inc("st", s.dma_start(out[(2 * t) * 128:(2 * t + 1) * 128, :],
                                      lnt[:, 0:D]), 16)
                inc("st", s.dma_start(out[(2 * t + 1) * 128:(2 * t + 2) * 128, :],
                                      lnt[:, D:2 * D]), 16)
            s.wait_ge(SEM["st"], CNT["st"])

        @block.vector
        def _(v):
            for t in range(NT // 2):
                t0 = lnt[:, 0:D]
                t1 = lnt[:, D:2 * D]
                v.wait_ge(SEM["ld"], l_ld0 + 16 * (t + 1))
                v.reduce_sum(tcol[:, 0:1], t0, AX.X)
                v.wait_ge(SEM["ld2"], l_ld20 + 16 * (t + 1))
                v.reduce_sum(tcol[:, 1:2], t1, AX.X)
                v.tensor_scalar(tcol[:, 0:1], tcol[:, 0:1], INV_D, None, ALU.mult)
                v.tensor_scalar(tcol[:, 1:2], tcol[:, 1:2], INV_D, None, ALU.mult)
                sep(v)
                inc("da", v.tensor_scalar(t0, t0, tcol[:, 0:1], None, ALU.subtract))
                inc("da", v.tensor_scalar(t1, t1, tcol[:, 1:2], None, ALU.subtract))
                v.wait_ge(SEM["aa"], l_a0 + 2 * (t + 1))
                v.reciprocal(tcol[:, 2:3], tcol[:, 2:3])
                v.reciprocal(tcol[:, 3:4], tcol[:, 3:4])
                sep(v)
                inc("da", v.tensor_scalar(t0, t0, tcol[:, 2:3], None, ALU.mult))
                inc("da", v.tensor_scalar(t1, t1, tcol[:, 3:4], None, ALU.mult))

        @block.scalar
        def _(sc):
            for t in range(NT // 2):
                t0 = lnt[:, 0:D]
                t1 = lnt[:, D:2 * D]
                sc.wait_ge(SEM["da"], l_d0 + 4 * t + 1)
                sc.activation(lns[:], t0, AFT.Square, bias=0.0, scale=1.0,
                              accum_out=tcol[:, 2:3])
                sc.wait_ge(SEM["da"], l_d0 + 4 * t + 2)
                sc.activation(lns[:], t1, AFT.Square, bias=0.0, scale=1.0,
                              accum_out=tcol[:, 3:4])
                ins = sc.activation(tcol[:, 2:3], tcol[:, 2:3], AFT.Sqrt,
                                    bias=EPS, scale=INV_D)
                ins.then_inc(SEM["aa"], 1)
                ins = sc.activation(tcol[:, 3:4], tcol[:, 3:4], AFT.Sqrt,
                                    bias=EPS, scale=INV_D)
                ins.then_inc(SEM["aa"], 1)
                sc.wait_ge(SEM["da"], l_d0 + 4 * t + 4)
        CNT["aa"] += NT
    es5.close()
    es0.close()
    return nc


def _host_layouts(x, w_gate, w_up, w_down):
    B, N, Dm = x.shape
    xf = np.ascontiguousarray(x.reshape(B * N, Dm))
    idx = np.arange(B * N).reshape(-1, 512)
    deint = np.concatenate([idx[:, 0::2], idx[:, 1::2]], axis=1).reshape(-1)
    xd_full = np.ascontiguousarray(xf[deint])

    wgt = np.ascontiguousarray(
        w_gate.T.reshape(HC, 128, NOG, 512).transpose(2, 0, 1, 3))
    wut = np.ascontiguousarray(
        w_up.T.reshape(HC, 128, NOG, 512).transpose(2, 0, 1, 3))
    wdt = np.ascontiguousarray(
        w_down.T.reshape(OC, 128, NDG, 256).transpose(2, 0, 1, 3))

    h = np.array([[1.0]], dtype=np.float64)
    for _ in range(7):
        h = np.block([[h, h], [h, -h]])
    h128 = h.astype(ml_dtypes.bfloat16)
    idn = np.eye(128, dtype=ml_dtypes.bfloat16)
    idnf = np.eye(128, dtype=np.float32)
    ones_r = np.ones((1, 128), np.float32)
    ones_c = np.ones((128, 1), np.float32)
    return xd_full, deint, wgt, wut, wdt, h128, idn, idnf, ones_r, ones_c


def kernel(L0_attn_x, L0_ffn_w_gate, L0_ffn_w_up, L0_ffn_w_down):
    x = np.asarray(L0_attn_x)
    B, N, Dm = x.shape
    xd_full, deint, wgt, wut, wdt, h128, idn, idnf, ones_r, ones_c = \
        _host_layouts(x, np.asarray(L0_ffn_w_gate), np.asarray(L0_ffn_w_up),
                      np.asarray(L0_ffn_w_down))

    nc = build_program(_STAGE)
    in_maps = []
    for c in range(8):
        xd_c = np.ascontiguousarray(xd_full[c * T:(c + 1) * T])
        in_maps.append(dict(
            xd=xd_c, xtr=np.ascontiguousarray(xd_c.T),
            wgt=wgt, wut=wut, wdt=wdt, h128=h128, idn=idn, idnf=idnf,
            ones_r=ones_r, ones_c=ones_c))
    res = run_bass_kernel_spmd(nc, in_maps, core_ids=list(range(8)))
    of = np.concatenate([r["out"] for r in res.results], axis=0)
    inv = np.empty_like(deint)
    inv[deint] = np.arange(B * N)
    return np.ascontiguousarray(of[inv].reshape(B, N, Dm))
